# version 16
# baseline (speedup 1.0000x reference)
import math
import numpy as np
import concourse.bass as bass
import concourse.mybir as mybir
from concourse.bass_utils import run_bass_kernel_spmd

F32 = mybir.dt.float32
BF16 = mybir.dt.bfloat16
I32 = mybir.dt.int32
AF = mybir.ActivationFunctionType
ALU = mybir.AluOpType
AX = mybir.AxisListType

C = 1024
J = 2816
NCC = 8
NJT = 22
QSC = float(np.float32(0.125 * math.sqrt(0.125)))
KSC = float(np.float32(math.sqrt(0.125)))
EPS = 1e-5
N_CORES = 8


def sl(i, n=128):
    return slice(i * n, (i + 1) * n)


class Eng:
    def __init__(self, eng, semh, serialize):
        self.e = eng
        self.sem = semh
        self.n = 0
        self.serialize = serialize

    def wait(self, other, val):
        if val > 0:
            self.e.wait_ge(other.sem, val)

    def wait_dma(self, dmaq, n):
        if n > 0:
            self.e.wait_ge(dmaq.sem, 16 * n)

    def op(self, name, *a, **k):
        if self.serialize and self.n > 0:
            self.e.wait_ge(self.sem, self.n)
        r = getattr(self.e, name)(*a, **k)
        r.then_inc(self.sem, 1)
        self.n += 1
        return self.n


class DmaQ:

    def __init__(self, eng):
        self.e = eng

    def wait(self, other, val):
        if val > 0:
            self.e.wait_ge(other.sem, val)

    def wait_dma(self, stream, n):
        if n > 0:
            self.e.wait_ge(stream.sem, 16 * n)


class Stream:

    def __init__(self, q, semh):
        self.q = q
        self.sem = semh
        self.n = 0

    def dma(self, out, in_, **k):
        self.q.e.dma_start(out=out, in_=in_, **k).then_inc(self.sem, 16)
        self.n += 1
        return self.n


class Builder:
    def __init__(self, nc, T):
        self.nc = nc
        self.T = T
        self.NT = T // 128
        self.NSUP = T // 512
        self.NFS = T // 1024
        self._tr_free = [0, 0]
        self._sg_free = 0
        self._wcast_free_pe = [0, 0]
        self._psb0_free_ve = 0
        self._psb0_free_ac = 0
        self._psb1_free_ve = 0
        self._qp_free_ve = 0
        self._qp_free_ac = 0
        self._ao_free = None
        self.aosb_free_ve = [0, 0]
        self._expb_free_pe = [0, 0]
        self._sim_free_ac = [0, 0]
        self._dsw_free = 0
        self._dtsw_free = 0
        self._rb_free = 0
        self._ln_ac_free = 0
        self._ln_ve_free = 0
        self._htr_free = 0
        self._ld_dt = 0
        self.wffn_parts = []
        self._pq_small_free = 0

    def tile_barrier(self):
        gp = self.pe.n
        self.ve.wait(self.pe, gp)
        self.ac.wait(self.pe, gp)

    def quant(self, v_ap, out_ap, F, v_ve=0, v_dma=0, out_gate_pe=0,
              eng=None, pq=False):
        ve, s = self.ve, self.s
        e = eng if eng is not None else ve
        pfx = "pq_" if pq else "qs_"
        nb = F // 16
        aa = s[pfx + "aa"][:, :F]
        cx = s[pfx + "c0"][:, :F]
        cy = s[pfx + "c1"][:, :F]
        mm = s[pfx + "m"][:, 0, :nb]
        sc = s[pfx + "m"][:, 1, :nb]
        rcp = s[pfx + "m"][:, 2, :nb]
        dqs = s[pfx + "d"][:, :nb]
        v3 = v_ap.rearrange("p (b i) -> p b i", i=16)

        if v_dma:
            ve.wait_dma(v_dma[0], v_dma[1])
        if e is not ve:
            ve.wait(e, self._pq_small_free)
        ve.op("tensor_reduce", mm, v3, AX.X, ALU.max,
              apply_absolute_value=True)
        ve.op("tensor_scalar", sc, mm, s["rcp6"][:, 0:1], 1e-12,
              ALU.mult, ALU.max)
        ve.op("reciprocal", rcp, sc)
        nved = ve.op("tensor_scalar", dqs, sc, 0.5, None, ALU.mult)
        if e is not ve:
            e.wait(ve, nved)
            if v_ve:
                e.wait(ve, v_ve)
        e.op("tensor_tensor", cx.rearrange("p (b i) -> p b i", i=16), v3,
             rcp.unsqueeze(2).broadcast_to([128, nb, 16]), ALU.mult)
        if e is ve:
            e.op("tensor_scalar", aa.bitcast(I32), cx.bitcast(I32),
                 s["imask"][:, 0:1], None, ALU.bitwise_and)
        else:
            e.op("tensor_scalar", aa.bitcast(I32), cx.bitcast(I32),
                 -2147483648.0, None, ALU.bitwise_xor)
            e.op("tensor_tensor", aa, aa, cx, ALU.max)
        e.op("tensor_scalar", cx, aa, 0.25, None, ALU.is_gt)
        e.op("scalar_tensor_tensor", cy, aa, 0.75, cx, ALU.is_gt, ALU.add)
        e.op("scalar_tensor_tensor", cx, aa, 1.25, cy, ALU.is_gt, ALU.add)
        e.op("scalar_tensor_tensor", cy, aa, 1.75, cx, ALU.is_gt, ALU.add)
        e.op("tensor_scalar", cx, aa, 5.0, 2.0, ALU.is_gt, ALU.mult)
        e.op("scalar_tensor_tensor", cx, aa, 3.5, cx, ALU.is_gt, ALU.add)
        e.op("scalar_tensor_tensor", cx, aa, 2.5, cx, ALU.is_gt, ALU.add)
        e.op("scalar_tensor_tensor", cy, cx, 2.0, cy, ALU.mult, ALU.add)
        smk = (s["smask"][:, 0:1] if e is ve else -2147483648.0)
        e.op("scalar_tensor_tensor", cx.bitcast(I32), v_ap.bitcast(I32),
             smk, cy.bitcast(I32), ALU.bitwise_and,
             ALU.bitwise_or)
        if out_gate_pe:
            e.wait(self.pe, out_gate_pe)
        n = e.op("tensor_tensor",
                 out_ap.rearrange("p (b i) -> p b i", i=16),
                 cx.rearrange("p (b i) -> p b i", i=16),
                 dqs.unsqueeze(2).broadcast_to([128, nb, 16]), ALU.mult)
        if e is not ve:
            self._pq_small_free = n
        return n

    def ln_apply(self, v_ap, out_ap, F):
        ve, ac, s = self.ve, self.ac, self.s
        ng = (F + 511) // 512
        ve.wait(ac, self._ln_ac_free)
        for gi in range(ng):
            ve.op("bn_stats", s["bn6"][:, gi, :], v_ap[:, sl(gi, 512)])
        nv = ve.op("bn_aggr", s["st2"][:], s["bn6"][:, :ng, :])
        ac.wait(ve, nv)
        self._ln_ac_free = ac.op("activation", s["lnv"][:], s["st2"][:, 1:2],
                                 AF.Ln, bias=s["epsT"][:, 0:1])
        ac.wait(ve, self._ln_ve_free)
        na = ac.op("activation", s["rstd"][:], s["lnv"][:], AF.Exp, scale=-0.5)
        ve.wait(ac, na)
        napp = ve.op("tensor_scalar", out_ap, v_ap, s["st2"][:, 0:1],
                     s["rstd"][:], ALU.subtract, ALU.mult)
        self._ln_ve_free = napp
        return napp

    def transpose_chunks(self, src_ap, nch, dst_fn, pe_gate_ve,
                         gate_ac=None, gate_po=None):
        pe, ve, s = self.pe, self.ve, self.s
        pe.wait(ve, pe_gate_ve)
        if gate_ac is not None:
            pe.wait(self.ac, gate_ac)
        if gate_po is not None:
            pe.wait(self.po, gate_po)
        trA = s["ps_o"][:, 3, 0:64].bitcast(BF16)
        trB = s["ps_o"][:, 2, 0:64].bitcast(BF16)
        for ch in range(nch):
            tgt = trA if ch % 2 == 0 else trB
            pe.wait(ve, self._tr_free[ch % 2])
            npt = pe.op("transpose", tgt[:, 0:128], src_ap[:, sl(ch)],
                        s["identb"][:])
            ve.wait(pe, npt)
            self._tr_free[ch % 2] = ve.op("tensor_copy", dst_fn(ch),
                                          tgt[:, 0:128])
        return ve.n, pe.n

    def emit_w(self):
        pe, ve, ac, po, ld, stq = (self.pe, self.ve, self.ac, self.po,
                                   self.ld, self.stq)
        s, g, st_ = self.s, self.g, self.streams

        po.op("memset", s["epsT"][:], EPS)
        po.op("memset", s["tiny1"][:], 1e-30)
        po.op("memset", s["imask"][:, 0:1], 0x7FFFFFFF)
        po.op("memset", s["smask"][:, 0:1], -2147483648)
        po.op("memset", s["identb"][:], 0.0)
        po.op("affine_select", out=s["identb"][:], in_=s["identb"][:],
              compare_op=ALU.not_equal, fill=1.0, base=0,
              pattern=[[-1, 128]], channel_multiplier=1)
        po.op("memset", s["triL"][:], 1.0)
        po.op("affine_select", out=s["triL"][:], in_=s["triL"][:],
              compare_op=ALU.is_ge, fill=0.0, base=0,
              pattern=[[1, 128]], channel_multiplier=-1)
        po.op("memset", s["negb"][:], -1e30)
        for j in range(4):
            po.op("affine_select", out=s["negb"][:, j, :],
                  in_=s["negb"][:, j, :], compare_op=ALU.is_ge, fill=0.0,
                  base=128 * j - 1, pattern=[[-1, 512]], channel_multiplier=1)
        po.op("memset", s["VhX"][:, :, 64:128], 1.0)
        po.op("memset", s["KT"][64:65, :], 1.0)
        po.op("memset", s["ones64"][:], 1.0)
        po.op("memset", s["ones128"][:], 1.0)
        po.op("memset", s["k2r"][:], 0.0)
        po.op("memset", s["rcp6"][:, 1:2], 6.0)
        n_po_w = po.n
        ve.wait(po, n_po_w)
        ve.op("reciprocal", s["rcp6"][:, 0:1], s["rcp6"][:, 1:2])

        for i, t in enumerate([g["lng_d"], g["lnb_d"], g["lnog_d"],
                               g["lnob_d"]]):
            st_["Lmisc"].dma(s["greps"][:, i, :],
                             t.unsqueeze(0).broadcast_to([128, C]))
        self._n_greps = st_["Lmisc"].n

        ve.wait(po, n_po_w)
        ac.wait(po, n_po_w)
        pe.wait(po, n_po_w)
        self._wtile_free = 0

        def cast_transpose(src_dram, row_tiles, dst, scale_rows=None):
            for rt in range(row_tiles):
                ld.wait(ve, self._wtile_free)
                nl = st_["Lwt"].dma(s["wtile"][:, :C], src_dram[sl(rt), :])
                ve.wait_dma(st_["Lwt"], nl)
                if scale_rows is None:
                    ncst = ve.op("tensor_copy", s["wcast"][:, 0, :C],
                                 s["wtile"][:, :C])
                else:
                    lo, hi, sc_ = scale_rows
                    ncst = ve.op("tensor_scalar", s["wcast"][lo:hi, 0, :C],
                                 s["wtile"][lo:hi, :C], sc_, None, ALU.mult)
                    if hi < 128:
                        ncst = ve.op("tensor_copy", s["wcast"][hi:128, 0, :C],
                                     s["wtile"][hi:128, :C])
                self._wtile_free = ncst
                _, npe0 = self.transpose_chunks(
                    s["wcast"][:, 0, :C], NCC, lambda cc: dst[:, cc, sl(rt)],
                    ncst)
                self._wcast_free_pe[0] = npe0

        cast_transpose(g["wq_d"], NCC, s["w_qT"], scale_rows=(0, 128, QSC))
        cast_transpose(g["wkv_d"], 1, s["w_kvT"], scale_rows=(0, 64, KSC))
        cast_transpose(g["wo_d"], NCC, s["w_oT"])

        self._wstage_free = 0
        self._wtile_free_po = 0

        def w_tile_ve(src, rt, rowlen, dstT):
            wslot = self._wslot
            self._wslot ^= 1
            ld.wait(ve, self._wtile_free_po)
            nl = st_["Lwt"].dma(s["wtile"][:, :rowlen], src[sl(rt), :])
            ve.wait(pe, self._wcast_free_pe[wslot])
            for off in range(0, rowlen, 1024):
                w = min(1024, rowlen - off)
                self.quant(s["wtile"][:, off:off + w],
                           s["wcast"][:, wslot, off:off + w], w,
                           v_dma=(st_["Lwt"], nl))
            self._wtile_free_po = ve.n
            return (rt, rowlen, dstT, wslot, ve.n)

        def w_tile_pe(pend):
            rt, rowlen, dstT, wslot, nq = pend
            nch = rowlen // 128
            for glo in range(0, nch, 8):
                ghi = min(glo + 8, nch)
                ve.wait_dma(st_["Swsp"], self._wstage_free)
                nev, npe = self.transpose_chunks(
                    s["wcast"][:, wslot, glo * 128:ghi * 128], ghi - glo,
                    lambda ch: s["xqTst"][:, 2, ch, :], nq)
                self._wcast_free_pe[wslot] = npe
                stq.wait(ve, nev)
                self._wstage_free = st_["Swsp"].dma(
                    dstT[:, glo:ghi, sl(rt)],
                    s["xqTst"][:, 2, 0:ghi - glo, :])

        self._w_tile_ve = w_tile_ve
        self._w_tile_pe = w_tile_pe
        self._w_pend = None
        self._wslot = 0
        self.wffn_parts = (
            [(g["wg_d"], rt, C, g["wqgT_s"]) for rt in range(NJT)] +
            [(g["wu_d"], rt, C, g["wquT_s"]) for rt in range(NJT)] +
            [(g["wd_d"], rt, J, g["wqdT_s"]) for rt in range(NCC)])

    def p0_ve_part(self, t):
        ve, ld = self.ve, self.ld
        s, g, st_ = self.s, self.g, self.streams
        slot = t % 2
        ld.wait(ve, self.xt_free[slot])
        nx = st_["Lx%d" % slot].dma(s["x_t"][:, slot, :], g["x_d"][sl(t), :])
        ve.wait_dma(st_["Lx%d" % slot], nx)
        nxn = self.ln_apply(s["x_t"][:, slot, :], s["xn_t"][:], C)
        self.xt_free[slot] = nxn
        self.quant(s["xn_t"][:], s["xq_t"][:, slot, :], C, v_ve=nxn,
                   out_gate_pe=self._xqt_pe[slot])
        self.ln_apply(s["xn_t"][:], s["t0f"][:], C)
        ve.op("tensor_tensor", s["t0f"][:], s["t0f"][:],
              s["greps"][:, 0, :], ALU.mult)
        ve.wait(self.pe, self._xnlt_pe[slot])
        return ve.op("tensor_tensor", s["xnl_t"][:, slot, :], s["t0f"][:],
                     s["greps"][:, 1, :], ALU.add)

    def p0_pe_part(self, t, nxnl):
        pe, ve, ac, ld, stq = self.pe, self.ve, self.ac, self.ld, self.stq
        s, g, st_ = self.s, self.g, self.streams
        ps_o = s["ps_o"]
        slot = t % 2
        tt = t % 4
        nq = nxnl
        ve.wait_dma(st_["Sxst%d" % slot], self.xst_free[slot])
        nev, self._xqt_pe[slot] = self.transpose_chunks(
            s["xq_t"][:, slot, :], NCC,
            lambda cc, _b=slot: s["xqTst"][:, _b, cc, :], nq)
        stq.wait(ve, nev)
        self.xst_free[slot] = st_["Sxst%d" % slot].dma(
            g["xqT_s"][:, :, sl(t)], s["xqTst"][:, slot, :, :])
        nevl, self._xnlt_pe[slot] = self.transpose_chunks(
            s["xnl_t"][:, slot, :], NCC,
            lambda cc, _tt=tt: s["xnlT"][:, cc, sl(_tt)], nxnl)
        kvp = ps_o[:, 2, 128:256]
        pe.wait(ve, nevl)
        pe.wait(ve, self._tr_free[1])
        for cc in range(NCC):
            nkv = pe.op("matmul", kvp, s["w_kvT"][:, cc, :],
                        s["xnlT"][:, cc, sl(tt)],
                        start=(cc == 0), stop=(cc == NCC - 1))
        ac.wait(pe, nkv)
        nktc = ac.op("activation", s["KT"][0:64, sl(t)], kvp[0:64, :],
                     AF.Copy)
        nvt = ac.op("activation", s["vtmp"][:], kvp[64:128, :], AF.Copy)
        ac.wait(pe, self._ksq_free_pe)
        nks = ac.op("activation", s["ksq"][:], s["KT"][0:64, sl(t)],
                    AF.Square)
        k2p = ps_o[0:1, 2, 256:384]
        pe.wait(ac, nks)
        pe.wait(ve, self._k2_evac)
        nk2m = pe.op("matmul", k2p, s["ones64"][:, 0:1], s["ksq"][:],
                     start=True, stop=True)
        self._ksq_free_pe = nk2m
        ve.wait(pe, nk2m)
        ve.op("reduce_max", s["smalls"][0:1, 0:1], k2p, AX.X)
        ve.wait(ac, self._k2r_ac)
        self._k2_evac = ve.op(
            "tensor_scalar", s["k2r"][0:1, 0:1], s["k2r"][0:1, 0:1],
            s["smalls"][0:1, 0:1], None, ALU.max)
        self.n_k2 = self._k2_evac
        vtr = ps_o[:, 2, 384:448].bitcast(BF16)
        pe.wait(ve, self._k2_evac)
        npv = pe.op("transpose", vtr[:, 0:64], s["vtmp"][:],
                    s["identb"][0:64, 0:64])
        ve.wait(pe, npv)
        ve.op("tensor_copy", s["VhX"][:, t, 0:64], vtr[:, 0:64])
        self._tr_free[1] = ve.op("tensor_copy", s["VhX"][:, t, 128:192],
                                 vtr[:, 0:64])

    def emit_qhat(self, sup):
        pe, ve, ac, ld, stq = self.pe, self.ve, self.ac, self.ld, self.stq
        s, g, st_ = self.s, self.g, self.streams
        ps_a, ps_b = s["ps_a"], s["ps_b"]
        ac.wait(ve, self.n_k2)
        nlnk2 = ac.op("activation", s["k2r"][0:1, 1:2],
                      s["k2r"][0:1, 0:1], AF.Ln,
                      bias=s["tiny1"][0:1, 0:1])
        self._k2r_ac = nlnk2
        qn_evac = 0
        pe.wait(ac, self._sim_free_ac[0])
        pe.wait(ac, self._sim_free_ac[1])
        for dd in range(NCC):
            qp = ps_a[:, 0, :]
            pe.wait(ac, self._qp_free_ve)
            for cc in range(NCC):
                nqm = pe.op("matmul", qp, s["w_qT"][:, cc, sl(dd)],
                            s["xnlT"][:, cc, :],
                            start=(cc == 0), stop=(cc == NCC - 1))
            ac.wait(pe, nqm)
            ac.op("activation", s["QT"][0:64, 2 * dd, :], qp[0:64, :],
                  AF.Copy)
            nqe = ac.op("activation", s["QT"][0:64, 2 * dd + 1, :],
                        qp[64:128, :], AF.Copy)
            self._qp_free_ve = nqe
            ac.wait(pe, self._qsq_free_pe)
            nqs = ac.op(
                "activation", s["qsq"][:].rearrange("p (h q) -> p h q",
                                                    q=512),
                s["QT"][0:64, 2 * dd:2 * dd + 2, :], AF.Square)
            qnp = ps_b[0:1, 0, :]
            pe.wait(ac, nqs)
            pe.wait(ve, self._psb0_free_ve)
            pe.op("matmul", qnp, s["ones64"][:, 0:1],
                  s["qsq"][:, 0:512], start=True, stop=False)
            nqn = pe.op("matmul", qnp, s["ones64"][:, 0:1],
                        s["qsq"][:, 512:1024], start=False, stop=True)
            self._qsq_free_pe = nqn
            ve.wait(pe, nqn)
            if dd == 0:
                ve.wait_dma(st_["Smr"], self.nmd)
            qn_evac = ve.op("tensor_copy", s["qn"][0:1, dd, :], qnp)
            self._psb0_free_ve = qn_evac
        qf = s["qn"][0:1, :, :].rearrange("a b c -> a (b c)")
        ac.wait(ve, qn_evac)
        nmr1 = ac.op("activation", qf, qf, AF.Ln,
                     bias=s["tiny1"][0:1, 0:1])
        ve.wait(ac, max(nmr1, nlnk2))
        nmr2 = ve.op("tensor_scalar", qf, qf, s["k2r"][0:1, 1:2], None,
                     ALU.add)
        ac.wait(ve, nmr2)
        nmr3 = ac.op("activation", qf, qf, AF.Exp, scale=0.5)
        ve.wait(ac, nmr3)
        nmrb = ve.op("tensor_scalar", qf, qf, -1.0, None, ALU.mult)
        stq.wait(ve, nmrb)
        stq.wait_dma(st_["Lmr"], self.nmq)
        self.nmd = st_["Smr"].dma(g["mrow_d"].unsqueeze(0),
                                  s["qn"][0:1, :, :])
        ld.wait_dma(st_["Smr"], self.nmd)
        ld.wait(pe, self._pe_after_sweeps)
        self.nmq = st_["Lmr"].dma(
            s["QT"][64:65, :, :].rearrange("o (p h) q -> o p h q", h=2),
            g["mrow_d"].unsqueeze(1).broadcast_to([NCC, 2, 512])
            .unsqueeze(0))
        self._preA_ve = ve.n

    def emit_sweeps(self, sup, inject=None):
        pe, ve, ac, ld, stq = self.pe, self.ve, self.ac, self.ld, self.stq
        po = self.po
        s, g, st_ = self.s, self.g, self.streams
        ps_a, ps_b, ps_o = s["ps_a"], s["ps_b"], s["ps_o"]
        kvmax = 4 * (sup + 1)
        pe.wait_dma(st_["Lmr"], self.nmq)
        sv_ve = self._preA_ve
        for w in range(NCC):
            if inject is not None and w < len(inject) and inject[w]:
                inject[w]()
            h0, h1 = 2 * w, 2 * w + 1
            ao0 = ps_o[:, 0, :]
            ao1 = ps_o[:, 1, :]
            npvm = 0
            pend = None

            def emit_pv(k, nex, pv_gate_ve):
                pe.wait(ac, nex)
                if pv_gate_ve:
                    pe.wait(ve, pv_gate_ve)
                if k == 0:
                    _e, _n = self._ao_free
                    pe.wait(_e, _n)
                ebuf = s["expb"][:, k % 2, :, :]
                pe.op("matmul", ao0, s["VhX"][:, k, 0:128],
                      ebuf[:, 0, :], start=(k == 0),
                      stop=(k == kvmax - 1), skip_group_check=True)
                n = pe.op("matmul", ao1, s["VhX"][:, k, 64:192],
                          ebuf[:, 1, :], start=(k == 0),
                          stop=(k == kvmax - 1), skip_group_check=True)
                self._expb_free_pe[k % 2] = n
                return n

            for k in range(kvmax):
                simp = ps_a if k % 2 == 0 else ps_b
                pe.wait(ac, self._sim_free_ac[k % 2])
                if w == 0 and k < 2:
                    pe.wait(ve, sv_ve)
                diag = k >= 4 * sup
                jj = k - 4 * sup
                pe.op("matmul", simp[:, 0, :], s["KT"][:, sl(k)],
                      s["QT"][:, h0, :], start=True, stop=not diag)
                if diag:
                    pe.op("matmul", simp[:, 0, :], s["triL"][:],
                          s["negb"][:, jj, :], start=False, stop=True)
                pe.op("matmul", simp[:, 1, :], s["KT"][:, sl(k)],
                      s["QT"][:, h1, :], start=True, stop=not diag)
                if diag:
                    pe.op("matmul", simp[:, 1, :], s["triL"][:],
                          s["negb"][:, jj, :], start=False, stop=True)
                nqk = pe.n
                if pend is not None:
                    npvm = emit_pv(*pend)
                ac.wait(pe, nqk)
                ac.wait(pe, self._expb_free_pe[k % 2])
                ebuf = s["expb"][:, k % 2, :, :]
                nex = ac.op("activation",
                            ebuf.rearrange("p h q -> p (h q)"),
                            simp[:, :, :].rearrange("p h q -> p (h q)"),
                            AF.Exp)
                self._sim_free_ac[k % 2] = nex
                pend = (k, nex, 0)
            npvm = emit_pv(*pend)
            wsl = w % 2
            aob = s["aosb"][:, wsl, :, :]
            ac.wait(pe, npvm)
            ac.wait(ve, self.aosb_free_ve[wsl])
            ac.wait_dma(st_["Sdsw"], self._dsw_free)
            ac.op("activation", aob[:, 0, :], ao0, AF.Copy)
            naosb = ac.op("activation", aob[:, 1, :], ao1, AF.Copy)
            self._ao_free = (ac, naosb)
            ac.op("activation", s["rb"][64:128, :], aob[64:128, 0, :],
                  AF.Ln)
            ac.op("activation", s["rb"][0:64, :], aob[0:64, 1, :], AF.Ln)
            nrp = ac.op("activation", s["rb"][:], s["rb"][:], AF.Exp,
                        scale=-1.0)
            stq.wait(ac, nrp)
            stq.wait_dma(st_["Ldt"], self._ld_dt)
            st_["Sdsw"].dma(g["dsw_d"][64:128, :], s["rb"][64:128, :])
            ndw = st_["Sdsw"].dma(g["dsw_d"][0:64, :], s["rb"][0:64, :])
            self._dsw_free = ndw
            ld.wait_dma(st_["Sdsw"], ndw)
            ld.wait(ve, self._dtsw_free)
            st_["Ldt"].dma(s["dtsw"][0:64, :], g["dsw_d"][64:128, :])
            ndr = st_["Ldt"].dma(s["dtsw"][64:128, :],
                                 g["dsw_d"][0:64, :])
            self._ld_dt = ndr
            ve.wait_dma(st_["Ldt"], ndr)
            ve.wait(ac, naosb)
            ve.op("tensor_tensor", s["AOn"][0:64, w, :],
                  aob[0:64, 0, :], s["dtsw"][0:64, :], ALU.mult)
            naon = ve.op("tensor_tensor", s["AOn"][64:128, w, :],
                         aob[64:128, 1, :], s["dtsw"][64:128, :],
                         ALU.mult)
            self._dtsw_free = naon
            self.aosb_free_ve[wsl] = naon

        aon_all = ve.n
        for sub in range(4):
            gt = sup * 4 + sub
            op0 = ps_o[:, 0, :]
            op1 = ps_o[:, 1, :]
            pe.wait(ve, aon_all)
            pe.wait(ve, self._op_free_ve)
            for ee in range(NCC):
                pe.op("matmul", op0, s["AOn"][:, ee, sl(sub)],
                      s["w_oT"][:, ee, 0:512],
                      start=(ee == 0), stop=(ee == NCC - 1),
                      skip_group_check=True)
                nop = pe.op("matmul", op1, s["AOn"][:, ee, sl(sub)],
                            s["w_oT"][:, ee, 512:1024],
                            start=(ee == 0), stop=(ee == NCC - 1),
                            skip_group_check=True)
            ve.wait(pe, nop)
            opv = ps_o[:, 0:2, :].rearrange("p a b -> p (a b)")
            self.ln_apply(opv, s["t0f"][:], C)
            ve.op("tensor_tensor", s["t0f"][:], s["t0f"][:],
                  s["greps"][:, 2, :], ALU.mult)
            ve.wait_dma(st_["Ss%d" % (sub % 2)], self.sst_free[sub % 2])
            ns = ve.op("tensor_tensor", s["st_s"][:, sub % 2, :],
                       s["t0f"][:], s["greps"][:, 3, :], ALU.add)
            self._op_free_ve = ns
            stq.wait(ve, ns)
            self.sst_free[sub % 2] = st_["Ss%d" % (sub % 2)].dma(
                g["s_s"][sl(gt), :], s["st_s"][:, sub % 2, :])
        self._ao_free = (ve, self._op_free_ve)
        self._pe_after_sweeps = pe.n

    def emit_a(self):
        pe, ve, ac, po, ld, stq = (self.pe, self.ve, self.ac, self.po,
                                   self.ld, self.stq)
        s, st_ = self.s, self.streams
        NSUP = self.NSUP

        ve.wait_dma(st_["Lmisc"], self._n_greps)
        ve.wait_dma(st_["Swsp"], st_["Swsp"].n)
        self.xt_free = [0, 0]
        self.xst_free = [0, 0]
        self.sst_free = [0, 0]
        self._vtr_free = 0
        self._k2_evac = 0
        self._k2r_ac = 0
        self._op_free_ve = 0
        self._xqt_pe = [0, 0]
        self._xnlt_pe = [0, 0]
        self._pe_after_sweeps = 0
        self._preA_ve = 0
        self._ao_free = (ve, 0)
        self.aosb_free_ve = [0, 0]
        self.nmd = 0
        self.nmq = 0
        self.n_k2 = 0

        for sup in range(NSUP + 1):
            if sup == 0:
                for t in range(4):
                    nx = self.p0_ve_part(t)
                    self.p0_pe_part(t, nx)
            if sup >= 1:
                base = 4 * sup
                nx_hold = {}
                wq = {NSUP - 3: 8, NSUP - 2: 16,
                      NSUP - 1: 16, NSUP: 12}.get(sup, 0) if NSUP >= 4 else 7

                def mk_both(ttv, ttp, nw, _b=base, _h=nx_hold,
                            _last=(sup == NSUP)):
                    def f():
                        if not _last:
                            if ttv is not None:
                                _h[ttv] = self.p0_ve_part(_b + ttv)
                            if ttp is not None:
                                self.p0_pe_part(_b + ttp, _h[ttp])
                        for _ in range(nw):
                            if self._w_pend is not None:
                                self._w_tile_pe(self._w_pend)
                                self._w_pend = None
                            if self.wffn_parts:
                                self._w_pend = self._w_tile_ve(
                                    *self.wffn_parts.pop(0))
                    return f

                def nsl(i):
                    return wq * (i + 1) // 8 - wq * i // 8

                inject = [mk_both(0, None, nsl(0)), mk_both(1, 0, nsl(1)),
                          mk_both(2, 1, nsl(2)), mk_both(3, 2, nsl(3)),
                          mk_both(None, 3, nsl(4)), mk_both(None, None, nsl(5)),
                          mk_both(None, None, nsl(6)),
                          mk_both(None, None, nsl(7))]
                self.emit_sweeps(sup - 1, inject)
            if sup < NSUP:
                self.emit_qhat(sup)
        while self.wffn_parts or self._w_pend is not None:
            if self._w_pend is not None:
                self._w_tile_pe(self._w_pend)
                self._w_pend = None
            if self.wffn_parts:
                self._w_pend = self._w_tile_ve(*self.wffn_parts.pop(0))

    def emit_f(self):
        pe, ve, ac, po, ld, stq = (self.pe, self.ve, self.ac, self.po,
                                   self.ld, self.stq)
        s, g, st_ = self.s, self.g, self.streams
        ps_a, ps_b, ps_o = s["ps_a"], s["ps_b"], s["ps_o"]
        NFS = self.NFS

        bar_pe, bar_ve, bar_ac = pe.n, ve.n, ac.n
        ld.wait(pe, bar_pe)
        ld.wait(ve, bar_ve)
        ld.wait(ac, bar_ac)
        for nm in ["Swsp", "Sxst0", "Sxst1", "Ss0", "Ss1", "Smr", "Sdsw"]:
            ld.wait_dma(st_[nm], st_[nm].n)
        pe.wait(ve, bar_ve)
        pe.wait(ac, bar_ac)
        ve.wait(pe, bar_pe)
        ac.wait(pe, bar_pe)

        nwd = st_["Lmisc"].dma(s["wqd_sb"][:], g["wqdT_s"][:, :, :])
        hq_read_pe = 0
        xq_read_pe = 0
        wg_free = [0, 0]
        xr_free = [0, 0]
        sld_free = [0, 0]
        o_free = [0, 0]
        g_done = 0

        hq_pe = [0, 0]
        for S in range(NFS):
            ld.wait(pe, xq_read_pe)
            nxq = st_["Lxqs"].dma(s["xqT_S"][:], g["xqT_s"][:, :, sl(S, 1024)])
            silu_done = 0
            tt_done = 0
            pend = None
            first_evac = True
            ui = 0
            for jc in range(6):
                jw = 512 if jc < 5 else 256
                jb = jc % 2
                ld.wait(pe, wg_free[jb])
                st_["Lwg%d" % jb].dma(
                    s["wgb"][:, jb, :, :jw],
                    g["wqgT_s"][:, :, jc * 512:jc * 512 + jw])
                nwu = st_["Lwg%d" % jb].dma(
                    s["wub"][:, jb, :, :jw],
                    g["wquT_s"][:, :, jc * 512:jc * 512 + jw])
                pe.wait_dma(st_["Lwg%d" % jb], nwu)
                if jc == 0:
                    pe.wait_dma(st_["Lxqs"], nxq)
                for sub in range(8):
                    slot = ui % 2
                    gp = ps_a[:, 0, :jw]
                    up = ps_b[:, 0, :jw]
                    pe.wait(ac, silu_done)
                    pe.wait(ve, tt_done)
                    for cc in range(NCC):
                        pe.op("matmul", gp, s["xqT_S"][:, cc, sl(sub)],
                              s["wgb"][:, jb, cc, :jw],
                              start=(cc == 0), stop=(cc == NCC - 1),
                              skip_group_check=True)
                        ngu = pe.op("matmul", up, s["xqT_S"][:, cc, sl(sub)],
                                    s["wub"][:, jb, cc, :jw],
                                    start=(cc == 0), stop=(cc == NCC - 1),
                                    skip_group_check=True)
                    wg_free[jb] = ngu
                    xq_read_pe = ngu
                    npt_prev = 0
                    if pend is not None:
                        pslot, pjw, pnq, pdst = pend
                        trC = ps_a[:, 1, 0:256].bitcast(BF16)
                        pe.wait(ve, pnq)
                        pe.wait(ac, self._htr_free)
                        for ji in range(pjw // 128):
                            npt_prev = pe.op("transpose", trC[:, sl(ji)],
                                             s["hq"][:, pslot, sl(ji)],
                                             s["identb"][:])
                        hq_pe[pslot] = npt_prev
                    ac.wait(pe, ngu)
                    ac.wait(ve, tt_done)
                    silu_done = ac.op("activation", s["hg"][:, :jw], gp,
                                      AF.Silu)
                    ve.wait(ac, silu_done)
                    ntt = ve.op("tensor_tensor", s["hsb"][:, :jw],
                                s["hg"][:, :jw], up, ALU.mult)
                    tt_done = ntt
                    nq = self.quant(s["hsb"][:, :jw], s["hq"][:, slot, :jw],
                                    jw, v_ve=ntt, out_gate_pe=hq_pe[slot])
                    if pend is not None:
                        pslot, pjw, pnq, pdst = pend
                        trC = ps_a[:, 1, 0:256].bitcast(BF16)
                        if first_evac:
                            ac.wait(pe, hq_read_pe)
                            first_evac = False
                        ac.wait(pe, npt_prev)
                        self._htr_free = ac.op(
                            "activation", pdst,
                            trC[:, 0:pjw].rearrange("p (a b) -> p a b",
                                                    b=128), AF.Copy)
                    pend = (slot, jw,  nq,
                            s["hqT"][:, jc * 4:jc * 4 + jw // 128, sl(sub)])
                    ui += 1
            pslot, pjw, pnq, pdst = pend
            trC = ps_a[:, 1, 0:256].bitcast(BF16)
            pe.wait(ve, pnq)
            pe.wait(ac, self._htr_free)
            for ji in range(pjw // 128):
                npt = pe.op("transpose", trC[:, sl(ji)],
                            s["hq"][:, pslot, sl(ji)], s["identb"][:])
            hq_pe[pslot] = npt
            if first_evac:
                ac.wait(pe, hq_read_pe)
            ac.wait(pe, npt)
            self._htr_free = ac.op(
                "activation", pdst,
                trC[:, 0:pjw].rearrange("p (a b) -> p a b", b=128), AF.Copy)
            nhq = ac.n
            pe.wait(ac, nhq)
            pe.wait_dma(st_["Lmisc"], nwd)
            for sub in range(8):
                gt = S * 8 + sub
                fb = sub % 2
                ld.wait(po, xr_free[fb])
                ld.wait(po, sld_free[fb])
                st_["Lxr%d" % fb].dma(s["xr"][:, fb, :], g["x_d"][sl(gt), :])
                nsl = st_["Lxr%d" % fb].dma(s["s_ld"][:, fb, :],
                                            g["s_s"][sl(gt), :])
                fp0 = ps_o[:, 0, :]
                fp1 = ps_o[:, 1, :]
                pe.wait(ac, g_done)
                for jcc in range(NJT):
                    pe.op("matmul", fp0, s["hqT"][:, jcc, sl(sub)],
                          s["wqd_sb"][:, jcc, 0:512],
                          start=(jcc == 0), stop=(jcc == NJT - 1),
                          skip_group_check=True)
                    nfm = pe.op("matmul", fp1, s["hqT"][:, jcc, sl(sub)],
                                s["wqd_sb"][:, jcc, 512:1024],
                                start=(jcc == 0), stop=(jcc == NJT - 1),
                                skip_group_check=True)
                hq_read_pe = nfm
                ac.wait(pe, nfm)
                ac.wait(po, sld_free[1 - fb])
                g_done = ac.op(
                    "activation", s["g_sb"][:],
                    ps_o[:, 0:2, :].rearrange("p a b -> p (a b)"), AF.Silu)
                po = self.po
                po.wait(ac, g_done)
                po.wait_dma(st_["Lxr%d" % fb], nsl)
                po.op("tensor_copy", s["t0f"][:], s["s_ld"][:, fb, :])
                po.op("tensor_tensor", s["t0f"][:], s["t0f"][:], s["g_sb"][:],
                      ALU.mult)
                sld_free[fb] = po.n
                po.wait_dma(st_["So%d" % fb], o_free[fb])
                no = po.op("tensor_tensor", s["o_sb"][:, fb, :], s["t0f"][:],
                           s["xr"][:, fb, :], ALU.add)
                xr_free[fb] = no
                stq.wait(po, no)
                o_free[fb] = st_["So%d" % fb].dma(g["out_d"][sl(gt), :],
                                                  s["o_sb"][:, fb, :])
        self.nc.sync.wait_ge(st_["So0"].sem, 16 * st_["So0"].n)
        self.nc.sync.wait_ge(st_["So1"].sem, 16 * st_["So1"].n)


def build_nc(T, phases="WAF"):
    nc = bass.Bass("TRN2", target_bir_lowering=False)
    b = Builder(nc, T)
    b.phases = phases[:3] if phases[:1] else phases
    if phases.endswith("n"):
        b.nointer = True
        phases = phases[:-1]
    if len(phases) > 2 and phases[2:3].isdigit():
        b.astage = int(phases[2])
        b.phases = "WA"
        if "t" in phases:
            b.tstage = int(phases.split("t")[1])
    else:
        b.phases = phases
    NT = b.NT

    g = {}
    g["x_d"] = nc.dram_tensor("L0_attn_x", [T, C], F32, kind="ExternalInput").ap()
    g["wq_d"] = nc.dram_tensor("L0_attn_w_q", [C, C], F32, kind="ExternalInput").ap()
    g["wkv_d"] = nc.dram_tensor("L0_attn_w_kv", [128, C], F32, kind="ExternalInput").ap()
    g["wo_d"] = nc.dram_tensor("L0_attn_w_out", [C, C], F32, kind="ExternalInput").ap()
    g["lng_d"] = nc.dram_tensor("L0_attn_ln_g", [C], F32, kind="ExternalInput").ap()
    g["lnb_d"] = nc.dram_tensor("L0_attn_ln_b", [C], F32, kind="ExternalInput").ap()
    g["lnog_d"] = nc.dram_tensor("L0_attn_lno_g", [C], F32, kind="ExternalInput").ap()
    g["lnob_d"] = nc.dram_tensor("L0_attn_lno_b", [C], F32, kind="ExternalInput").ap()
    g["wg_d"] = nc.dram_tensor("L0_ffn_w_gate", [J, C], F32, kind="ExternalInput").ap()
    g["wu_d"] = nc.dram_tensor("L0_ffn_w_up", [J, C], F32, kind="ExternalInput").ap()
    g["wd_d"] = nc.dram_tensor("L0_ffn_w_down", [C, J], F32, kind="ExternalInput").ap()
    g["out_d"] = nc.dram_tensor("out", [T, C], F32, kind="ExternalOutput").ap()
    g["wqgT_s"] = nc.dram_tensor("wqgT_s", [128, NCC, J], BF16, kind="Internal").ap()
    g["wquT_s"] = nc.dram_tensor("wquT_s", [128, NCC, J], BF16, kind="Internal").ap()
    g["wqdT_s"] = nc.dram_tensor("wqdT_s", [128, NJT, C], BF16, kind="Internal").ap()
    g["xqT_s"] = nc.dram_tensor("xqT_s", [128, NCC, T], BF16, kind="Internal").ap()
    g["s_s"] = nc.dram_tensor("s_s", [T, C], BF16, kind="Internal").ap()
    g["mrow_d"] = nc.dram_tensor("mrow_d", [NCC, 512], BF16, kind="Internal").ap()
    g["dsw_d"] = nc.dram_tensor("dsw_d", [128, 512], F32, kind="Internal").ap()

    from contextlib import ExitStack
    with ExitStack() as st:
        def sb(name, shape, dt=F32):
            return st.enter_context(nc.sbuf_tensor(name, shape, dt))

        def psum(name, shape, dt=F32):
            return st.enter_context(nc.psum_tensor(name, shape, dt))

        s = {}
        s["epsT"] = sb("epsT", [128, 1])
        s["tiny1"] = sb("tiny1", [1, 1])
        s["imask"] = sb("imask", [128, 1], I32)
        s["smask"] = sb("smask", [128, 1], I32)
        s["identb"] = sb("identb", [128, 128], BF16)
        s["triL"] = sb("triL", [128, 128], BF16)
        s["negb"] = sb("negb", [128, 4, 512], BF16)
        s["ones64"] = sb("ones64", [64, 1], BF16)
        s["ones128"] = sb("ones128", [128, 1], BF16)
        s["smalls"] = sb("smalls", [1, 8])
        s["rcp6"] = sb("rcp6", [128, 2])
        s["k2r"] = sb("k2r", [1, 2])
        s["bn6"] = sb("bn6", [128, 2, 6])
        s["st2"] = sb("st2", [128, 2])
        s["lnv"] = sb("lnv", [128, 1])
        s["rstd"] = sb("rstd", [128, 1])
        s["t0f"] = sb("t0f", [128, C])
        s["ps_a"] = psum("ps_a", [128, 2, 512])
        s["ps_b"] = psum("ps_b", [128, 2, 512])
        s["ps_o"] = psum("ps_o", [128, 4, 512])

        def sem(name):
            return st.enter_context(nc.semaphore(name))

        block = st.enter_context(nc.Block())
        b.pe = Eng(nc.tensor, sem("pe_ch"), serialize=False)
        b.ve = Eng(nc.vector, sem("ve_ch"), serialize=True)
        b.ac = Eng(nc.scalar, sem("ac_ch"), serialize=True)
        b.po = Eng(nc.gpsimd, sem("po_ch"), serialize=True)
        b.ld = DmaQ(nc.sync)
        b.stq = DmaQ(nc.scalar)
        ldq, stqq = b.ld, b.stq
        b.streams = {}
        for nm in ["Lmisc", "Lwt", "Lx0", "Lx1", "Ldt", "Lmr", "Lxqs",
                   "Lwg0", "Lwg1", "Lxr0", "Lxr1"]:
            b.streams[nm] = Stream(ldq, sem(nm))
        for nm in ["Swsp", "Sxst0", "Sxst1", "Ss0", "Ss1", "Smr", "Sdsw",
                   "So0", "So1"]:
            b.streams[nm] = Stream(stqq, sem(nm))
        if True:
            b.s = s
            b.g = g
            b._ksq_free_pe = 0
            b._qsq_free_pe = 0

            @block.sync
            def _(sync):
                ph = getattr(b, "phases", "WAF")
                with ExitStack() as stA:
                    def sba(name, shape, dt=F32):
                        return stA.enter_context(
                            nc.sbuf_tensor(name, shape, dt))
                    s["greps"] = sba("greps", [128, 4, C])
                    s["qs_aa"] = sba("qs_aa", [128, 1024])
                    s["qs_c0"] = sba("qs_c0", [128, 1024])
                    s["qs_c1"] = sba("qs_c1", [128, 1024])
                    s["qs_m"] = sba("qs_m", [128, 3, 64])
                    s["qs_d"] = sba("qs_d", [128, 64])
                    s["wtile"] = sba("wtile", [128, 2816])
                    s["wcast"] = sba("wcast", [128, 2, 2816], BF16)
                    s["w_qT"] = sba("w_qT", [128, NCC, C], BF16)
                    s["w_kvT"] = sba("w_kvT", [128, NCC, 128], BF16)
                    s["w_oT"] = sba("w_oT", [128, NCC, C], BF16)
                    s["KT"] = sba("KT", [65, T], BF16)
                    s["VhX"] = sba("VhX", [128, NT, 192], BF16)
                    s["QT"] = sba("QT", [65, 16, 512], BF16)
                    s["xnlT"] = sba("xnlT", [128, NCC, 512], BF16)
                    s["AOn"] = sba("AOn", [128, NCC, 512], BF16)
                    s["expb"] = sba("expb", [128, 2, 2, 512], BF16)
                    s["rb"] = sba("rb", [128, 512])
                    s["aosb"] = sba("aosb", [128, 2, 2, 512])
                    s["dtsw"] = sba("dtsw", [128, 512])
                    s["qsq"] = sba("qsq", [64, 1024], BF16)
                    s["ksq"] = sba("ksq", [64, 128], BF16)
                    s["qn"] = sba("qn", [1, NCC, 512], BF16)
                    s["x_t"] = sba("x_t", [128, 2, C])
                    s["xn_t"] = sba("xn_t", [128, C])
                    s["xnl_t"] = sba("xnl_t", [128, 2, C], BF16)
                    s["xq_t"] = sba("xq_t", [128, 2, C], BF16)
                    s["xqTst"] = sba("xqTst", [128, 3, NCC, 128], BF16)
                    s["vtmp"] = sba("vtmp", [64, 128], BF16)
                    s["st_s"] = sba("st_s", [128, 2, C], BF16)
                    b.emit_w()
                    if "A" in ph:
                        b.emit_a()
                with ExitStack() as stF:
                    def sbf(name, shape, dt=F32):
                        return stF.enter_context(
                            nc.sbuf_tensor(name, shape, dt))
                    s["qs_aa"] = sbf("f_qs_aa", [128, 512])
                    s["qs_c0"] = sbf("f_qs_c0", [128, 512])
                    s["qs_c1"] = sbf("f_qs_c1", [128, 512])
                    s["qs_m"] = sbf("f_qs_m", [128, 3, 32])
                    s["qs_d"] = sbf("f_qs_d", [128, 32])
                    s["wqd_sb"] = sbf("wqd_sb", [128, NJT, C], BF16)
                    s["hqT"] = sbf("hqT", [128, NJT, 1024], BF16)
                    s["xqT_S"] = sbf("xqT_S", [128, NCC, 1024], BF16)
                    s["wgb"] = sbf("wgb", [128, 2, NCC, 512], BF16)
                    s["wub"] = sbf("wub", [128, 2, NCC, 512], BF16)
                    s["hg"] = sbf("hg", [128, 512])
                    s["hsb"] = sbf("hsb", [128, 512])
                    s["hq"] = sbf("hq", [128, 2, 512], BF16)
                    s["g_sb"] = sbf("g_sb", [128, C])
                    s["s_ld"] = sbf("s_ld", [128, 2, C], BF16)
                    s["xr"] = sbf("xr", [128, 2, C])
                    s["o_sb"] = sbf("o_sb", [128, 2, C])
                    if "F" in ph:
                        b.emit_f()
                    else:
                        doq = b.streams["So0"]
                        for t in range(b.NT):
                            nc.scalar.wait_ge(b.ve.sem, b.ve.n)
                            doq.dma(g["out_d"][sl(t), :], g["x_d"][sl(t), :])
                        nc.sync.wait_ge(doq.sem, 16 * doq.n)

    return nc


def kernel(**inputs):
    T = 4096
    nc = build_nc(T)
    x = np.ascontiguousarray(np.asarray(inputs["L0_attn_x"], dtype=np.float32))
    weights = {k: np.ascontiguousarray(np.asarray(v, dtype=np.float32))
               for k, v in inputs.items() if k != "L0_attn_x"}
    in_maps = []
    for bb in range(N_CORES):
        m = {"L0_attn_x": x[bb]}
        m.update(weights)
        in_maps.append(m)
    res = run_bass_kernel_spmd(nc, in_maps, core_ids=list(range(N_CORES)))
    return np.stack([res.results[bb]["out"] for bb in range(N_CORES)], axis=0)


# version 17
# speedup vs baseline: 1.0293x; 1.0293x over previous
import math
import numpy as np
import concourse.bass as bass
import concourse.mybir as mybir
from concourse.bass_utils import run_bass_kernel_spmd

F32 = mybir.dt.float32
BF16 = mybir.dt.bfloat16
I32 = mybir.dt.int32
AF = mybir.ActivationFunctionType
ALU = mybir.AluOpType
AX = mybir.AxisListType

C = 1024
J = 2816
NCC = 8
NJT = 22
QSC = float(np.float32(0.125 * math.sqrt(0.125)))
KSC = float(np.float32(math.sqrt(0.125)))
EPS = 1e-5
N_CORES = 8


def sl(i, n=128):
    return slice(i * n, (i + 1) * n)


class Eng:
    def __init__(self, eng, semh, serialize):
        self.e = eng
        self.sem = semh
        self.n = 0
        self.serialize = serialize

    def wait(self, other, val):
        if val > 0:
            self.e.wait_ge(other.sem, val)

    def wait_dma(self, dmaq, n):
        if n > 0:
            self.e.wait_ge(dmaq.sem, 16 * n)

    def op(self, name, *a, **k):
        if self.serialize and self.n > 0:
            self.e.wait_ge(self.sem, self.n)
        r = getattr(self.e, name)(*a, **k)
        r.then_inc(self.sem, 1)
        self.n += 1
        return self.n


class DmaQ:

    def __init__(self, eng):
        self.e = eng

    def wait(self, other, val):
        if val > 0:
            self.e.wait_ge(other.sem, val)

    def wait_dma(self, stream, n):
        if n > 0:
            self.e.wait_ge(stream.sem, 16 * n)


class Stream:

    def __init__(self, q, semh):
        self.q = q
        self.sem = semh
        self.n = 0

    def dma(self, out, in_, **k):
        self.q.e.dma_start(out=out, in_=in_, **k).then_inc(self.sem, 16)
        self.n += 1
        return self.n


class Builder:
    def __init__(self, nc, T):
        self.nc = nc
        self.T = T
        self.NT = T // 128
        self.NSUP = T // 512
        self.NFS = T // 1024
        self._tr_free = [0, 0]
        self._sg_free = 0
        self._wcast_free_pe = [0, 0]
        self._psb0_free_ve = 0
        self._psb0_free_ac = 0
        self._psb1_free_ve = 0
        self._qp_free_ve = 0
        self._qp_free_ac = 0
        self._ao_free = None
        self.aosb_free_ve = [0, 0]
        self._expb_free_pe = [0, 0]
        self._sim_free_ac = [0, 0]
        self._dsw_free = 0
        self._dtsw_free = 0
        self._rb_free = 0
        self._ln_ac_free = 0
        self._ln_ve_free = 0
        self._htr_free = 0
        self._ld_dt = 0
        self.wffn_parts = []
        self._pq_small_free = 0

    def tile_barrier(self):
        gp = self.pe.n
        self.ve.wait(self.pe, gp)
        self.ac.wait(self.pe, gp)

    def quant(self, v_ap, out_ap, F, v_ve=0, v_dma=0, out_gate_pe=0,
              eng=None, pq=False):
        ve, s = self.ve, self.s
        e = eng if eng is not None else ve
        pfx = "pq_" if pq else "qs_"
        nb = F // 16
        aa = s[pfx + "aa"][:, :F]
        cx = s[pfx + "c0"][:, :F]
        cy = s[pfx + "c1"][:, :F]
        mm = s[pfx + "m"][:, 0, :nb]
        sc = s[pfx + "m"][:, 1, :nb]
        rcp = s[pfx + "m"][:, 2, :nb]
        dqs = s[pfx + "d"][:, :nb]
        v3 = v_ap.rearrange("p (b i) -> p b i", i=16)

        if v_dma:
            ve.wait_dma(v_dma[0], v_dma[1])
        if e is not ve:
            ve.wait(e, self._pq_small_free)
        ve.op("tensor_reduce", mm, v3, AX.X, ALU.max,
              apply_absolute_value=True)
        ve.op("tensor_scalar", sc, mm, s["rcp6"][:, 0:1], 1e-12,
              ALU.mult, ALU.max)
        ve.op("reciprocal", rcp, sc)
        nved = ve.op("tensor_scalar", dqs, sc, 0.5, None, ALU.mult)
        if e is not ve:
            e.wait(ve, nved)
            if v_ve:
                e.wait(ve, v_ve)
        e.op("tensor_tensor", cx.rearrange("p (b i) -> p b i", i=16), v3,
             rcp.unsqueeze(2).broadcast_to([128, nb, 16]), ALU.mult)
        if e is ve:
            e.op("tensor_scalar", aa.bitcast(I32), cx.bitcast(I32),
                 s["imask"][:, 0:1], None, ALU.bitwise_and)
        else:
            e.op("tensor_scalar", aa.bitcast(I32), cx.bitcast(I32),
                 -2147483648.0, None, ALU.bitwise_xor)
            e.op("tensor_tensor", aa, aa, cx, ALU.max)
        e.op("tensor_scalar", cx, aa, 0.25, None, ALU.is_gt)
        e.op("scalar_tensor_tensor", cy, aa, 0.75, cx, ALU.is_gt, ALU.add)
        e.op("scalar_tensor_tensor", cx, aa, 1.25, cy, ALU.is_gt, ALU.add)
        e.op("scalar_tensor_tensor", cy, aa, 1.75, cx, ALU.is_gt, ALU.add)
        e.op("tensor_scalar", cx, aa, 5.0, 2.0, ALU.is_gt, ALU.mult)
        e.op("scalar_tensor_tensor", cx, aa, 3.5, cx, ALU.is_gt, ALU.add)
        e.op("scalar_tensor_tensor", cx, aa, 2.5, cx, ALU.is_gt, ALU.add)
        e.op("scalar_tensor_tensor", cy, cx, 2.0, cy, ALU.mult, ALU.add)
        smk = (s["smask"][:, 0:1] if e is ve else -2147483648.0)
        e.op("scalar_tensor_tensor", cx.bitcast(I32), v_ap.bitcast(I32),
             smk, cy.bitcast(I32), ALU.bitwise_and,
             ALU.bitwise_or)
        if out_gate_pe:
            e.wait(self.pe, out_gate_pe)
        n = e.op("tensor_tensor",
                 out_ap.rearrange("p (b i) -> p b i", i=16),
                 cx.rearrange("p (b i) -> p b i", i=16),
                 dqs.unsqueeze(2).broadcast_to([128, nb, 16]), ALU.mult)
        if e is not ve:
            self._pq_small_free = n
        return n

    def ln_apply(self, v_ap, out_ap, F):
        ve, ac, s = self.ve, self.ac, self.s
        ng = (F + 511) // 512
        ve.wait(ac, self._ln_ac_free)
        for gi in range(ng):
            ve.op("bn_stats", s["bn6"][:, gi, :], v_ap[:, sl(gi, 512)])
        nv = ve.op("bn_aggr", s["st2"][:], s["bn6"][:, :ng, :])
        ac.wait(ve, nv)
        self._ln_ac_free = ac.op("activation", s["lnv"][:], s["st2"][:, 1:2],
                                 AF.Ln, bias=s["epsT"][:, 0:1])
        ac.wait(ve, self._ln_ve_free)
        na = ac.op("activation", s["rstd"][:], s["lnv"][:], AF.Exp, scale=-0.5)
        ve.wait(ac, na)
        napp = ve.op("tensor_scalar", out_ap, v_ap, s["st2"][:, 0:1],
                     s["rstd"][:], ALU.subtract, ALU.mult)
        self._ln_ve_free = napp
        return napp

    def transpose_chunks(self, src_ap, nch, dst_fn, pe_gate_ve,
                         gate_ac=None, gate_po=None):
        pe, ve, s = self.pe, self.ve, self.s
        pe.wait(ve, pe_gate_ve)
        if gate_ac is not None:
            pe.wait(self.ac, gate_ac)
        if gate_po is not None:
            pe.wait(self.po, gate_po)
        trA = s["ps_o"][:, 3, 0:64].bitcast(BF16)
        trB = s["ps_o"][:, 2, 0:64].bitcast(BF16)
        for ch in range(nch):
            tgt = trA if ch % 2 == 0 else trB
            pe.wait(ve, self._tr_free[ch % 2])
            npt = pe.op("transpose", tgt[:, 0:128], src_ap[:, sl(ch)],
                        s["identb"][:])
            ve.wait(pe, npt)
            self._tr_free[ch % 2] = ve.op("tensor_copy", dst_fn(ch),
                                          tgt[:, 0:128])
        return ve.n, pe.n

    def emit_w(self):
        pe, ve, ac, po, ld, stq = (self.pe, self.ve, self.ac, self.po,
                                   self.ld, self.stq)
        s, g, st_ = self.s, self.g, self.streams

        po.op("memset", s["epsT"][:], EPS)
        po.op("memset", s["tiny1"][:], 1e-30)
        po.op("memset", s["imask"][:, 0:1], 0x7FFFFFFF)
        po.op("memset", s["smask"][:, 0:1], -2147483648)
        po.op("memset", s["identb"][:], 0.0)
        po.op("affine_select", out=s["identb"][:], in_=s["identb"][:],
              compare_op=ALU.not_equal, fill=1.0, base=0,
              pattern=[[-1, 128]], channel_multiplier=1)
        po.op("memset", s["triL"][:], 1.0)
        po.op("affine_select", out=s["triL"][:], in_=s["triL"][:],
              compare_op=ALU.is_ge, fill=0.0, base=0,
              pattern=[[1, 128]], channel_multiplier=-1)
        po.op("memset", s["negb"][:], -1e30)
        for j in range(4):
            po.op("affine_select", out=s["negb"][:, j, :],
                  in_=s["negb"][:, j, :], compare_op=ALU.is_ge, fill=0.0,
                  base=128 * j - 1, pattern=[[-1, 512]], channel_multiplier=1)
        po.op("memset", s["VhX"][:, :, 64:128], 1.0)
        po.op("memset", s["KT"][64:65, :], 1.0)
        po.op("memset", s["ones64"][:], 1.0)
        po.op("memset", s["ones128"][:], 1.0)
        po.op("memset", s["k2r"][:], 0.0)
        po.op("memset", s["rcp6"][:, 1:2], 6.0)
        n_po_w = po.n
        ve.wait(po, n_po_w)
        ve.op("reciprocal", s["rcp6"][:, 0:1], s["rcp6"][:, 1:2])

        for i, t in enumerate([g["lng_d"], g["lnb_d"], g["lnog_d"],
                               g["lnob_d"]]):
            st_["Lmisc"].dma(s["greps"][:, i, :],
                             t.unsqueeze(0).broadcast_to([128, C]))
        self._n_greps = st_["Lmisc"].n

        ve.wait(po, n_po_w)
        ac.wait(po, n_po_w)
        pe.wait(po, n_po_w)
        self._wtile_free = 0

        def cast_transpose(src_dram, row_tiles, dst, scale_rows=None):
            for rt in range(row_tiles):
                ld.wait(ve, self._wtile_free)
                nl = st_["Lwt"].dma(s["wtile"][:, :C], src_dram[sl(rt), :])
                ve.wait_dma(st_["Lwt"], nl)
                if scale_rows is None:
                    ncst = ve.op("tensor_copy", s["wcast"][:, 0, :C],
                                 s["wtile"][:, :C])
                else:
                    lo, hi, sc_ = scale_rows
                    ncst = ve.op("tensor_scalar", s["wcast"][lo:hi, 0, :C],
                                 s["wtile"][lo:hi, :C], sc_, None, ALU.mult)
                    if hi < 128:
                        ncst = ve.op("tensor_copy", s["wcast"][hi:128, 0, :C],
                                     s["wtile"][hi:128, :C])
                self._wtile_free = ncst
                _, npe0 = self.transpose_chunks(
                    s["wcast"][:, 0, :C], NCC, lambda cc: dst[:, cc, sl(rt)],
                    ncst)
                self._wcast_free_pe[0] = npe0

        cast_transpose(g["wq_d"], NCC, s["w_qT"], scale_rows=(0, 128, QSC))
        cast_transpose(g["wkv_d"], 1, s["w_kvT"], scale_rows=(0, 64, KSC))
        cast_transpose(g["wo_d"], NCC, s["w_oT"])

        self._wstage_free = 0
        self._wtile_free_po = 0

        def w_tile_ve(src, rt, rowlen, dstT):
            wslot = self._wslot
            self._wslot ^= 1
            ld.wait(ve, self._wtile_free_po)
            nl = st_["Lwt"].dma(s["wtile"][:, :rowlen], src[sl(rt), :])
            ve.wait(pe, self._wcast_free_pe[wslot])
            for off in range(0, rowlen, 1024):
                w = min(1024, rowlen - off)
                self.quant(s["wtile"][:, off:off + w],
                           s["wcast"][:, wslot, off:off + w], w,
                           v_dma=(st_["Lwt"], nl))
            self._wtile_free_po = ve.n
            return (rt, rowlen, dstT, wslot, ve.n)

        def w_tile_pe(pend):
            rt, rowlen, dstT, wslot, nq = pend
            nch = rowlen // 128
            for glo in range(0, nch, 8):
                ghi = min(glo + 8, nch)
                ve.wait_dma(st_["Swsp"], self._wstage_free)
                nev, npe = self.transpose_chunks(
                    s["wcast"][:, wslot, glo * 128:ghi * 128], ghi - glo,
                    lambda ch: s["xqTst"][:, 2, ch, :], nq)
                self._wcast_free_pe[wslot] = npe
                stq.wait(ve, nev)
                self._wstage_free = st_["Swsp"].dma(
                    dstT[:, glo:ghi, sl(rt)],
                    s["xqTst"][:, 2, 0:ghi - glo, :])

        self._w_tile_ve = w_tile_ve
        self._w_tile_pe = w_tile_pe
        self._w_pend = []
        self._wslot = 0
        self.wffn_parts = (
            [(g["wg_d"], rt, C, g["wqgT_s"]) for rt in range(NJT)] +
            [(g["wu_d"], rt, C, g["wquT_s"]) for rt in range(NJT)] +
            [(g["wd_d"], rt, J, g["wqdT_s"]) for rt in range(NCC)])

    def p0_ve_part(self, t):
        ve, ld = self.ve, self.ld
        s, g, st_ = self.s, self.g, self.streams
        slot = t % 2
        ld.wait(ve, self.xt_free[slot])
        nx = st_["Lx%d" % slot].dma(s["x_t"][:, slot, :], g["x_d"][sl(t), :])
        ve.wait_dma(st_["Lx%d" % slot], nx)
        nxn = self.ln_apply(s["x_t"][:, slot, :], s["xn_t"][:], C)
        self.xt_free[slot] = nxn
        self.quant(s["xn_t"][:], s["xq_t"][:, slot, :], C, v_ve=nxn,
                   out_gate_pe=self._xqt_pe[slot])
        self.ln_apply(s["xn_t"][:], s["t0f"][:], C)
        ve.op("tensor_tensor", s["t0f"][:], s["t0f"][:],
              s["greps"][:, 0, :], ALU.mult)
        ve.wait(self.pe, self._xnlt_pe[slot])
        return ve.op("tensor_tensor", s["xnl_t"][:, slot, :], s["t0f"][:],
                     s["greps"][:, 1, :], ALU.add)

    def p0_pe_part(self, t, nxnl):
        pe, ve, ac, ld, stq = self.pe, self.ve, self.ac, self.ld, self.stq
        s, g, st_ = self.s, self.g, self.streams
        ps_o = s["ps_o"]
        slot = t % 2
        tt = t % 4
        nq = nxnl
        ve.wait_dma(st_["Sxst%d" % slot], self.xst_free[slot])
        nev, self._xqt_pe[slot] = self.transpose_chunks(
            s["xq_t"][:, slot, :], NCC,
            lambda cc, _b=slot: s["xqTst"][:, _b, cc, :], nq)
        stq.wait(ve, nev)
        self.xst_free[slot] = st_["Sxst%d" % slot].dma(
            g["xqT_s"][:, :, sl(t)], s["xqTst"][:, slot, :, :])
        nevl, self._xnlt_pe[slot] = self.transpose_chunks(
            s["xnl_t"][:, slot, :], NCC,
            lambda cc, _tt=tt: s["xnlT"][:, cc, sl(_tt)], nxnl)
        kvp = ps_o[:, 2, 128:256]
        pe.wait(ve, nevl)
        pe.wait(ve, self._tr_free[1])
        for cc in range(NCC):
            nkv = pe.op("matmul", kvp, s["w_kvT"][:, cc, :],
                        s["xnlT"][:, cc, sl(tt)],
                        start=(cc == 0), stop=(cc == NCC - 1))
        ac.wait(pe, nkv)
        nktc = ac.op("activation", s["KT"][0:64, sl(t)], kvp[0:64, :],
                     AF.Copy)
        nvt = ac.op("activation", s["vtmp"][:], kvp[64:128, :], AF.Copy)
        ac.wait(pe, self._ksq_free_pe)
        nks = ac.op("activation", s["ksq"][:], s["KT"][0:64, sl(t)],
                    AF.Square)
        k2p = ps_o[0:1, 2, 256:384]
        pe.wait(ac, nks)
        pe.wait(ve, self._k2_evac)
        nk2m = pe.op("matmul", k2p, s["ones64"][:, 0:1], s["ksq"][:],
                     start=True, stop=True)
        self._ksq_free_pe = nk2m
        ve.wait(pe, nk2m)
        ve.op("reduce_max", s["smalls"][0:1, 0:1], k2p, AX.X)
        ve.wait(ac, self._k2r_ac)
        self._k2_evac = ve.op(
            "tensor_scalar", s["k2r"][0:1, 0:1], s["k2r"][0:1, 0:1],
            s["smalls"][0:1, 0:1], None, ALU.max)
        self.n_k2 = self._k2_evac
        vtr = ps_o[:, 2, 384:448].bitcast(BF16)
        pe.wait(ve, self._k2_evac)
        npv = pe.op("transpose", vtr[:, 0:64], s["vtmp"][:],
                    s["identb"][0:64, 0:64])
        ve.wait(pe, npv)
        ve.op("tensor_copy", s["VhX"][:, t, 0:64], vtr[:, 0:64])
        self._tr_free[1] = ve.op("tensor_copy", s["VhX"][:, t, 128:192],
                                 vtr[:, 0:64])

    def emit_qhat(self, sup):
        pe, ve, ac, ld, stq = self.pe, self.ve, self.ac, self.ld, self.stq
        s, g, st_ = self.s, self.g, self.streams
        ps_a, ps_b = s["ps_a"], s["ps_b"]
        ac.wait(ve, self.n_k2)
        nlnk2 = ac.op("activation", s["k2r"][0:1, 1:2],
                      s["k2r"][0:1, 0:1], AF.Ln,
                      bias=s["tiny1"][0:1, 0:1])
        self._k2r_ac = nlnk2
        qn_evac = 0
        pe.wait(ac, self._sim_free_ac[0])
        pe.wait(ac, self._sim_free_ac[1])
        for dd in range(NCC):
            qp = ps_a[:, 0, :]
            pe.wait(ac, self._qp_free_ve)
            for cc in range(NCC):
                nqm = pe.op("matmul", qp, s["w_qT"][:, cc, sl(dd)],
                            s["xnlT"][:, cc, :],
                            start=(cc == 0), stop=(cc == NCC - 1))
            ac.wait(pe, nqm)
            ac.op("activation", s["QT"][0:64, 2 * dd, :], qp[0:64, :],
                  AF.Copy)
            nqe = ac.op("activation", s["QT"][0:64, 2 * dd + 1, :],
                        qp[64:128, :], AF.Copy)
            self._qp_free_ve = nqe
            ac.wait(pe, self._qsq_free_pe)
            nqs = ac.op(
                "activation", s["qsq"][:].rearrange("p (h q) -> p h q",
                                                    q=512),
                s["QT"][0:64, 2 * dd:2 * dd + 2, :], AF.Square)
            qnp = ps_b[0:1, 0, :]
            pe.wait(ac, nqs)
            pe.wait(ve, self._psb0_free_ve)
            pe.op("matmul", qnp, s["ones64"][:, 0:1],
                  s["qsq"][:, 0:512], start=True, stop=False)
            nqn = pe.op("matmul", qnp, s["ones64"][:, 0:1],
                        s["qsq"][:, 512:1024], start=False, stop=True)
            self._qsq_free_pe = nqn
            ve.wait(pe, nqn)
            if dd == 0:
                ve.wait_dma(st_["Smr"], self.nmd)
            qn_evac = ve.op("tensor_copy", s["qn"][0:1, dd, :], qnp)
            self._psb0_free_ve = qn_evac
        qf = s["qn"][0:1, :, :].rearrange("a b c -> a (b c)")
        ac.wait(ve, qn_evac)
        nmr1 = ac.op("activation", qf, qf, AF.Ln,
                     bias=s["tiny1"][0:1, 0:1])
        ve.wait(ac, max(nmr1, nlnk2))
        nmr2 = ve.op("tensor_scalar", qf, qf, s["k2r"][0:1, 1:2], None,
                     ALU.add)
        ac.wait(ve, nmr2)
        nmr3 = ac.op("activation", qf, qf, AF.Exp, scale=0.5)
        ve.wait(ac, nmr3)
        nmrb = ve.op("tensor_scalar", qf, qf, -1.0, None, ALU.mult)
        stq.wait(ve, nmrb)
        stq.wait_dma(st_["Lmr"], self.nmq)
        self.nmd = st_["Smr"].dma(g["mrow_d"].unsqueeze(0),
                                  s["qn"][0:1, :, :])
        ld.wait_dma(st_["Smr"], self.nmd)
        ld.wait(pe, self._pe_after_sweeps)
        self.nmq = st_["Lmr"].dma(
            s["QT"][64:65, :, :].rearrange("o (p h) q -> o p h q", h=2),
            g["mrow_d"].unsqueeze(1).broadcast_to([NCC, 2, 512])
            .unsqueeze(0))
        self._preA_ve = ve.n

    def emit_sweeps(self, sup, inject=None):
        pe, ve, ac, ld, stq = self.pe, self.ve, self.ac, self.ld, self.stq
        po = self.po
        s, g, st_ = self.s, self.g, self.streams
        ps_a, ps_b, ps_o = s["ps_a"], s["ps_b"], s["ps_o"]
        kvmax = 4 * (sup + 1)
        pe.wait_dma(st_["Lmr"], self.nmq)
        sv_ve = self._preA_ve
        for w in range(NCC):
            if inject is not None and w < len(inject) and inject[w]:
                inject[w]()
            h0, h1 = 2 * w, 2 * w + 1
            ao0 = ps_o[:, 0, :]
            ao1 = ps_o[:, 1, :]
            npvm = 0
            pend = None

            def emit_pv(k, nex, pv_gate_ve):
                pe.wait(ac, nex)
                if pv_gate_ve:
                    pe.wait(ve, pv_gate_ve)
                if k == 0:
                    _e, _n = self._ao_free
                    pe.wait(_e, _n)
                ebuf = s["expb"][:, k % 2, :, :]
                pe.op("matmul", ao0, s["VhX"][:, k, 0:128],
                      ebuf[:, 0, :], start=(k == 0),
                      stop=(k == kvmax - 1), skip_group_check=True)
                n = pe.op("matmul", ao1, s["VhX"][:, k, 64:192],
                          ebuf[:, 1, :], start=(k == 0),
                          stop=(k == kvmax - 1), skip_group_check=True)
                self._expb_free_pe[k % 2] = n
                return n

            for k in range(kvmax):
                simp = ps_a if k % 2 == 0 else ps_b
                pe.wait(ac, self._sim_free_ac[k % 2])
                if w == 0 and k < 2:
                    pe.wait(ve, sv_ve)
                diag = k >= 4 * sup
                jj = k - 4 * sup
                pe.op("matmul", simp[:, 0, :], s["KT"][:, sl(k)],
                      s["QT"][:, h0, :], start=True, stop=not diag)
                if diag:
                    pe.op("matmul", simp[:, 0, :], s["triL"][:],
                          s["negb"][:, jj, :], start=False, stop=True)
                pe.op("matmul", simp[:, 1, :], s["KT"][:, sl(k)],
                      s["QT"][:, h1, :], start=True, stop=not diag)
                if diag:
                    pe.op("matmul", simp[:, 1, :], s["triL"][:],
                          s["negb"][:, jj, :], start=False, stop=True)
                nqk = pe.n
                if pend is not None:
                    npvm = emit_pv(*pend)
                ac.wait(pe, nqk)
                ac.wait(pe, self._expb_free_pe[k % 2])
                ebuf = s["expb"][:, k % 2, :, :]
                nex = ac.op("activation",
                            ebuf.rearrange("p h q -> p (h q)"),
                            simp[:, :, :].rearrange("p h q -> p (h q)"),
                            AF.Exp)
                self._sim_free_ac[k % 2] = nex
                pend = (k, nex, 0)
            npvm = emit_pv(*pend)
            wsl = w % 2
            aob = s["aosb"][:, wsl, :, :]
            ac.wait(pe, npvm)
            ac.wait(ve, self.aosb_free_ve[wsl])
            ac.wait_dma(st_["Sdsw"], self._dsw_free)
            ac.op("activation", aob[:, 0, :], ao0, AF.Copy)
            naosb = ac.op("activation", aob[:, 1, :], ao1, AF.Copy)
            self._ao_free = (ac, naosb)
            ac.op("activation", s["rb"][64:128, :], aob[64:128, 0, :],
                  AF.Ln)
            ac.op("activation", s["rb"][0:64, :], aob[0:64, 1, :], AF.Ln)
            nrp = ac.op("activation", s["rb"][:], s["rb"][:], AF.Exp,
                        scale=-1.0)
            stq.wait(ac, nrp)
            stq.wait_dma(st_["Ldt"], self._ld_dt)
            st_["Sdsw"].dma(g["dsw_d"][64:128, :], s["rb"][64:128, :])
            ndw = st_["Sdsw"].dma(g["dsw_d"][0:64, :], s["rb"][0:64, :])
            self._dsw_free = ndw
            ld.wait_dma(st_["Sdsw"], ndw)
            ld.wait(ve, self._dtsw_free)
            st_["Ldt"].dma(s["dtsw"][0:64, :], g["dsw_d"][64:128, :])
            ndr = st_["Ldt"].dma(s["dtsw"][64:128, :],
                                 g["dsw_d"][0:64, :])
            self._ld_dt = ndr
            ve.wait_dma(st_["Ldt"], ndr)
            ve.wait(ac, naosb)
            ve.op("tensor_tensor", s["AOn"][0:64, w, :],
                  aob[0:64, 0, :], s["dtsw"][0:64, :], ALU.mult)
            naon = ve.op("tensor_tensor", s["AOn"][64:128, w, :],
                         aob[64:128, 1, :], s["dtsw"][64:128, :],
                         ALU.mult)
            self._dtsw_free = naon
            self.aosb_free_ve[wsl] = naon

        aon_all = ve.n
        for sub in range(4):
            gt = sup * 4 + sub
            op0 = ps_o[:, 0, :]
            op1 = ps_o[:, 1, :]
            pe.wait(ve, aon_all)
            pe.wait(ve, self._op_free_ve)
            for ee in range(NCC):
                pe.op("matmul", op0, s["AOn"][:, ee, sl(sub)],
                      s["w_oT"][:, ee, 0:512],
                      start=(ee == 0), stop=(ee == NCC - 1),
                      skip_group_check=True)
                nop = pe.op("matmul", op1, s["AOn"][:, ee, sl(sub)],
                            s["w_oT"][:, ee, 512:1024],
                            start=(ee == 0), stop=(ee == NCC - 1),
                            skip_group_check=True)
            ve.wait(pe, nop)
            opv = ps_o[:, 0:2, :].rearrange("p a b -> p (a b)")
            self.ln_apply(opv, s["t0f"][:], C)
            ve.op("tensor_tensor", s["t0f"][:], s["t0f"][:],
                  s["greps"][:, 2, :], ALU.mult)
            ve.wait_dma(st_["Ss%d" % (sub % 2)], self.sst_free[sub % 2])
            ns = ve.op("tensor_tensor", s["st_s"][:, sub % 2, :],
                       s["t0f"][:], s["greps"][:, 3, :], ALU.add)
            self._op_free_ve = ns
            stq.wait(ve, ns)
            self.sst_free[sub % 2] = st_["Ss%d" % (sub % 2)].dma(
                g["s_s"][sl(gt), :], s["st_s"][:, sub % 2, :])
        self._ao_free = (ve, self._op_free_ve)
        self._pe_after_sweeps = pe.n

    def emit_a(self):
        pe, ve, ac, po, ld, stq = (self.pe, self.ve, self.ac, self.po,
                                   self.ld, self.stq)
        s, st_ = self.s, self.streams
        NSUP = self.NSUP

        ve.wait_dma(st_["Lmisc"], self._n_greps)
        ve.wait_dma(st_["Swsp"], st_["Swsp"].n)
        self.xt_free = [0, 0]
        self.xst_free = [0, 0]
        self.sst_free = [0, 0]
        self._vtr_free = 0
        self._k2_evac = 0
        self._k2r_ac = 0
        self._op_free_ve = 0
        self._xqt_pe = [0, 0]
        self._xnlt_pe = [0, 0]
        self._pe_after_sweeps = 0
        self._preA_ve = 0
        self._ao_free = (ve, 0)
        self.aosb_free_ve = [0, 0]
        self.nmd = 0
        self.nmq = 0
        self.n_k2 = 0

        for sup in range(NSUP + 1):
            if sup == 0:
                for t in range(4):
                    nx = self.p0_ve_part(t)
                    self.p0_pe_part(t, nx)
            if sup >= 1:
                base = 4 * sup
                nx_hold = {}
                wq = {NSUP - 5: 6, NSUP - 4: 8, NSUP - 3: 8,
                      NSUP - 2: 10, NSUP - 1: 10,
                      NSUP: 10}.get(sup, 0) if NSUP >= 6 else 7

                def mk_both(ttv, ttp, nw, _b=base, _h=nx_hold,
                            _last=(sup == NSUP)):
                    def f():
                        if not _last:
                            if ttv is not None:
                                _h[ttv] = self.p0_ve_part(_b + ttv)
                            if ttp is not None:
                                self.p0_pe_part(_b + ttp, _h[ttp])
                        for _ in range(nw):
                            if len(self._w_pend) >= 2:
                                self._w_tile_pe(self._w_pend.pop(0))
                            if self.wffn_parts:
                                self._w_pend.append(self._w_tile_ve(
                                    *self.wffn_parts.pop(0)))
                    return f

                def nsl(i):
                    return wq * (i + 1) // 8 - wq * i // 8

                inject = [mk_both(0, None, nsl(0)), mk_both(1, 0, nsl(1)),
                          mk_both(2, 1, nsl(2)), mk_both(3, 2, nsl(3)),
                          mk_both(None, 3, nsl(4)), mk_both(None, None, nsl(5)),
                          mk_both(None, None, nsl(6)),
                          mk_both(None, None, nsl(7))]
                self.emit_sweeps(sup - 1, inject)
            if sup < NSUP:
                self.emit_qhat(sup)
        while self.wffn_parts or self._w_pend:
            if self._w_pend:
                self._w_tile_pe(self._w_pend.pop(0))
            if self.wffn_parts:
                self._w_pend.append(
                    self._w_tile_ve(*self.wffn_parts.pop(0)))

    def emit_f(self):
        pe, ve, ac, po, ld, stq = (self.pe, self.ve, self.ac, self.po,
                                   self.ld, self.stq)
        s, g, st_ = self.s, self.g, self.streams
        ps_a, ps_b, ps_o = s["ps_a"], s["ps_b"], s["ps_o"]
        NFS = self.NFS

        bar_pe, bar_ve, bar_ac = pe.n, ve.n, ac.n
        ld.wait(pe, bar_pe)
        ld.wait(ve, bar_ve)
        ld.wait(ac, bar_ac)
        for nm in ["Swsp", "Sxst0", "Sxst1", "Ss0", "Ss1", "Smr", "Sdsw"]:
            ld.wait_dma(st_[nm], st_[nm].n)
        pe.wait(ve, bar_ve)
        pe.wait(ac, bar_ac)
        ve.wait(pe, bar_pe)
        ac.wait(pe, bar_pe)

        nwd = st_["Lmisc"].dma(s["wqd_sb"][:], g["wqdT_s"][:, :, :])
        hq_read_pe = 0
        xq_read_pe = 0
        wg_free = [0, 0]
        xr_free = [0, 0]
        sld_free = [0, 0]
        o_free = [0, 0]
        g_done = 0

        hq_pe = [0, 0]
        for S in range(NFS):
            ld.wait(pe, xq_read_pe)
            nxq = st_["Lxqs"].dma(s["xqT_S"][:], g["xqT_s"][:, :, sl(S, 1024)])
            silu_done = 0
            tt_done = 0
            pend = None
            first_evac = True
            ui = 0
            for jc in range(6):
                jw = 512 if jc < 5 else 256
                jb = jc % 2
                ld.wait(pe, wg_free[jb])
                st_["Lwg%d" % jb].dma(
                    s["wgb"][:, jb, :, :jw],
                    g["wqgT_s"][:, :, jc * 512:jc * 512 + jw])
                nwu = st_["Lwg%d" % jb].dma(
                    s["wub"][:, jb, :, :jw],
                    g["wquT_s"][:, :, jc * 512:jc * 512 + jw])
                pe.wait_dma(st_["Lwg%d" % jb], nwu)
                if jc == 0:
                    pe.wait_dma(st_["Lxqs"], nxq)
                for sub in range(8):
                    slot = ui % 2
                    gp = ps_a[:, 0, :jw]
                    up = ps_b[:, 0, :jw]
                    pe.wait(ac, silu_done)
                    pe.wait(ve, tt_done)
                    for cc in range(NCC):
                        pe.op("matmul", gp, s["xqT_S"][:, cc, sl(sub)],
                              s["wgb"][:, jb, cc, :jw],
                              start=(cc == 0), stop=(cc == NCC - 1),
                              skip_group_check=True)
                        ngu = pe.op("matmul", up, s["xqT_S"][:, cc, sl(sub)],
                                    s["wub"][:, jb, cc, :jw],
                                    start=(cc == 0), stop=(cc == NCC - 1),
                                    skip_group_check=True)
                    wg_free[jb] = ngu
                    xq_read_pe = ngu
                    npt_prev = 0
                    if pend is not None:
                        pslot, pjw, pnq, pdst = pend
                        trC = ps_a[:, 1, 0:256].bitcast(BF16)
                        pe.wait(ve, pnq)
                        pe.wait(ac, self._htr_free)
                        for ji in range(pjw // 128):
                            npt_prev = pe.op("transpose", trC[:, sl(ji)],
                                             s["hq"][:, pslot, sl(ji)],
                                             s["identb"][:])
                        hq_pe[pslot] = npt_prev
                    ac.wait(pe, ngu)
                    ac.wait(ve, tt_done)
                    silu_done = ac.op("activation", s["hg"][:, :jw], gp,
                                      AF.Silu)
                    ve.wait(ac, silu_done)
                    ntt = ve.op("tensor_tensor", s["hsb"][:, :jw],
                                s["hg"][:, :jw], up, ALU.mult)
                    tt_done = ntt
                    nq = self.quant(s["hsb"][:, :jw], s["hq"][:, slot, :jw],
                                    jw, v_ve=ntt, out_gate_pe=hq_pe[slot])
                    if pend is not None:
                        pslot, pjw, pnq, pdst = pend
                        trC = ps_a[:, 1, 0:256].bitcast(BF16)
                        if first_evac:
                            ac.wait(pe, hq_read_pe)
                            first_evac = False
                        ac.wait(pe, npt_prev)
                        self._htr_free = ac.op(
                            "activation", pdst,
                            trC[:, 0:pjw].rearrange("p (a b) -> p a b",
                                                    b=128), AF.Copy)
                    pend = (slot, jw,  nq,
                            s["hqT"][:, jc * 4:jc * 4 + jw // 128, sl(sub)])
                    ui += 1
            pslot, pjw, pnq, pdst = pend
            trC = ps_a[:, 1, 0:256].bitcast(BF16)
            pe.wait(ve, pnq)
            pe.wait(ac, self._htr_free)
            for ji in range(pjw // 128):
                npt = pe.op("transpose", trC[:, sl(ji)],
                            s["hq"][:, pslot, sl(ji)], s["identb"][:])
            hq_pe[pslot] = npt
            if first_evac:
                ac.wait(pe, hq_read_pe)
            ac.wait(pe, npt)
            self._htr_free = ac.op(
                "activation", pdst,
                trC[:, 0:pjw].rearrange("p (a b) -> p a b", b=128), AF.Copy)
            nhq = ac.n
            pe.wait(ac, nhq)
            pe.wait_dma(st_["Lmisc"], nwd)
            for sub in range(8):
                gt = S * 8 + sub
                fb = sub % 2
                ld.wait(po, xr_free[fb])
                ld.wait(po, sld_free[fb])
                st_["Lxr%d" % fb].dma(s["xr"][:, fb, :], g["x_d"][sl(gt), :])
                nsl = st_["Lxr%d" % fb].dma(s["s_ld"][:, fb, :],
                                            g["s_s"][sl(gt), :])
                fp0 = ps_o[:, 0, :]
                fp1 = ps_o[:, 1, :]
                pe.wait(ac, g_done)
                for jcc in range(NJT):
                    pe.op("matmul", fp0, s["hqT"][:, jcc, sl(sub)],
                          s["wqd_sb"][:, jcc, 0:512],
                          start=(jcc == 0), stop=(jcc == NJT - 1),
                          skip_group_check=True)
                    nfm = pe.op("matmul", fp1, s["hqT"][:, jcc, sl(sub)],
                                s["wqd_sb"][:, jcc, 512:1024],
                                start=(jcc == 0), stop=(jcc == NJT - 1),
                                skip_group_check=True)
                hq_read_pe = nfm
                ac.wait(pe, nfm)
                ac.wait(po, sld_free[1 - fb])
                g_done = ac.op(
                    "activation", s["g_sb"][:],
                    ps_o[:, 0:2, :].rearrange("p a b -> p (a b)"), AF.Silu)
                po = self.po
                po.wait(ac, g_done)
                po.wait_dma(st_["Lxr%d" % fb], nsl)
                po.op("tensor_copy", s["t0f"][:], s["s_ld"][:, fb, :])
                po.op("tensor_tensor", s["t0f"][:], s["t0f"][:], s["g_sb"][:],
                      ALU.mult)
                sld_free[fb] = po.n
                po.wait_dma(st_["So%d" % fb], o_free[fb])
                no = po.op("tensor_tensor", s["o_sb"][:, fb, :], s["t0f"][:],
                           s["xr"][:, fb, :], ALU.add)
                xr_free[fb] = no
                stq.wait(po, no)
                o_free[fb] = st_["So%d" % fb].dma(g["out_d"][sl(gt), :],
                                                  s["o_sb"][:, fb, :])
        self.nc.sync.wait_ge(st_["So0"].sem, 16 * st_["So0"].n)
        self.nc.sync.wait_ge(st_["So1"].sem, 16 * st_["So1"].n)


def build_nc(T, phases="WAF"):
    nc = bass.Bass("TRN2", target_bir_lowering=False)
    b = Builder(nc, T)
    b.phases = phases[:3] if phases[:1] else phases
    if phases.endswith("n"):
        b.nointer = True
        phases = phases[:-1]
    if len(phases) > 2 and phases[2:3].isdigit():
        b.astage = int(phases[2])
        b.phases = "WA"
        if "t" in phases:
            b.tstage = int(phases.split("t")[1])
    else:
        b.phases = phases
    NT = b.NT

    g = {}
    g["x_d"] = nc.dram_tensor("L0_attn_x", [T, C], F32, kind="ExternalInput").ap()
    g["wq_d"] = nc.dram_tensor("L0_attn_w_q", [C, C], F32, kind="ExternalInput").ap()
    g["wkv_d"] = nc.dram_tensor("L0_attn_w_kv", [128, C], F32, kind="ExternalInput").ap()
    g["wo_d"] = nc.dram_tensor("L0_attn_w_out", [C, C], F32, kind="ExternalInput").ap()
    g["lng_d"] = nc.dram_tensor("L0_attn_ln_g", [C], F32, kind="ExternalInput").ap()
    g["lnb_d"] = nc.dram_tensor("L0_attn_ln_b", [C], F32, kind="ExternalInput").ap()
    g["lnog_d"] = nc.dram_tensor("L0_attn_lno_g", [C], F32, kind="ExternalInput").ap()
    g["lnob_d"] = nc.dram_tensor("L0_attn_lno_b", [C], F32, kind="ExternalInput").ap()
    g["wg_d"] = nc.dram_tensor("L0_ffn_w_gate", [J, C], F32, kind="ExternalInput").ap()
    g["wu_d"] = nc.dram_tensor("L0_ffn_w_up", [J, C], F32, kind="ExternalInput").ap()
    g["wd_d"] = nc.dram_tensor("L0_ffn_w_down", [C, J], F32, kind="ExternalInput").ap()
    g["out_d"] = nc.dram_tensor("out", [T, C], F32, kind="ExternalOutput").ap()
    g["wqgT_s"] = nc.dram_tensor("wqgT_s", [128, NCC, J], BF16, kind="Internal").ap()
    g["wquT_s"] = nc.dram_tensor("wquT_s", [128, NCC, J], BF16, kind="Internal").ap()
    g["wqdT_s"] = nc.dram_tensor("wqdT_s", [128, NJT, C], BF16, kind="Internal").ap()
    g["xqT_s"] = nc.dram_tensor("xqT_s", [128, NCC, T], BF16, kind="Internal").ap()
    g["s_s"] = nc.dram_tensor("s_s", [T, C], BF16, kind="Internal").ap()
    g["mrow_d"] = nc.dram_tensor("mrow_d", [NCC, 512], BF16, kind="Internal").ap()
    g["dsw_d"] = nc.dram_tensor("dsw_d", [128, 512], F32, kind="Internal").ap()

    from contextlib import ExitStack
    with ExitStack() as st:
        def sb(name, shape, dt=F32):
            return st.enter_context(nc.sbuf_tensor(name, shape, dt))

        def psum(name, shape, dt=F32):
            return st.enter_context(nc.psum_tensor(name, shape, dt))

        s = {}
        s["epsT"] = sb("epsT", [128, 1])
        s["tiny1"] = sb("tiny1", [1, 1])
        s["imask"] = sb("imask", [128, 1], I32)
        s["smask"] = sb("smask", [128, 1], I32)
        s["identb"] = sb("identb", [128, 128], BF16)
        s["triL"] = sb("triL", [128, 128], BF16)
        s["negb"] = sb("negb", [128, 4, 512], BF16)
        s["ones64"] = sb("ones64", [64, 1], BF16)
        s["ones128"] = sb("ones128", [128, 1], BF16)
        s["smalls"] = sb("smalls", [1, 8])
        s["rcp6"] = sb("rcp6", [128, 2])
        s["k2r"] = sb("k2r", [1, 2])
        s["bn6"] = sb("bn6", [128, 2, 6])
        s["st2"] = sb("st2", [128, 2])
        s["lnv"] = sb("lnv", [128, 1])
        s["rstd"] = sb("rstd", [128, 1])
        s["t0f"] = sb("t0f", [128, C])
        s["ps_a"] = psum("ps_a", [128, 2, 512])
        s["ps_b"] = psum("ps_b", [128, 2, 512])
        s["ps_o"] = psum("ps_o", [128, 4, 512])

        def sem(name):
            return st.enter_context(nc.semaphore(name))

        block = st.enter_context(nc.Block())
        b.pe = Eng(nc.tensor, sem("pe_ch"), serialize=False)
        b.ve = Eng(nc.vector, sem("ve_ch"), serialize=True)
        b.ac = Eng(nc.scalar, sem("ac_ch"), serialize=True)
        b.po = Eng(nc.gpsimd, sem("po_ch"), serialize=True)
        b.ld = DmaQ(nc.sync)
        b.stq = DmaQ(nc.scalar)
        ldq, stqq = b.ld, b.stq
        b.streams = {}
        for nm in ["Lmisc", "Lwt", "Lx0", "Lx1", "Ldt", "Lmr", "Lxqs",
                   "Lwg0", "Lwg1", "Lxr0", "Lxr1"]:
            b.streams[nm] = Stream(ldq, sem(nm))
        for nm in ["Swsp", "Sxst0", "Sxst1", "Ss0", "Ss1", "Smr", "Sdsw",
                   "So0", "So1"]:
            b.streams[nm] = Stream(stqq, sem(nm))
        if True:
            b.s = s
            b.g = g
            b._ksq_free_pe = 0
            b._qsq_free_pe = 0

            @block.sync
            def _(sync):
                ph = getattr(b, "phases", "WAF")
                with ExitStack() as stA:
                    def sba(name, shape, dt=F32):
                        return stA.enter_context(
                            nc.sbuf_tensor(name, shape, dt))
                    s["greps"] = sba("greps", [128, 4, C])
                    s["qs_aa"] = sba("qs_aa", [128, 1024])
                    s["qs_c0"] = sba("qs_c0", [128, 1024])
                    s["qs_c1"] = sba("qs_c1", [128, 1024])
                    s["qs_m"] = sba("qs_m", [128, 3, 64])
                    s["qs_d"] = sba("qs_d", [128, 64])
                    s["wtile"] = sba("wtile", [128, 2816])
                    s["wcast"] = sba("wcast", [128, 2, 2816], BF16)
                    s["w_qT"] = sba("w_qT", [128, NCC, C], BF16)
                    s["w_kvT"] = sba("w_kvT", [128, NCC, 128], BF16)
                    s["w_oT"] = sba("w_oT", [128, NCC, C], BF16)
                    s["KT"] = sba("KT", [65, T], BF16)
                    s["VhX"] = sba("VhX", [128, NT, 192], BF16)
                    s["QT"] = sba("QT", [65, 16, 512], BF16)
                    s["xnlT"] = sba("xnlT", [128, NCC, 512], BF16)
                    s["AOn"] = sba("AOn", [128, NCC, 512], BF16)
                    s["expb"] = sba("expb", [128, 2, 2, 512], BF16)
                    s["rb"] = sba("rb", [128, 512])
                    s["aosb"] = sba("aosb", [128, 2, 2, 512])
                    s["dtsw"] = sba("dtsw", [128, 512])
                    s["qsq"] = sba("qsq", [64, 1024], BF16)
                    s["ksq"] = sba("ksq", [64, 128], BF16)
                    s["qn"] = sba("qn", [1, NCC, 512], BF16)
                    s["x_t"] = sba("x_t", [128, 2, C])
                    s["xn_t"] = sba("xn_t", [128, C])
                    s["xnl_t"] = sba("xnl_t", [128, 2, C], BF16)
                    s["xq_t"] = sba("xq_t", [128, 2, C], BF16)
                    s["xqTst"] = sba("xqTst", [128, 3, NCC, 128], BF16)
                    s["vtmp"] = sba("vtmp", [64, 128], BF16)
                    s["st_s"] = sba("st_s", [128, 2, C], BF16)
                    b.emit_w()
                    if "A" in ph:
                        b.emit_a()
                with ExitStack() as stF:
                    def sbf(name, shape, dt=F32):
                        return stF.enter_context(
                            nc.sbuf_tensor(name, shape, dt))
                    s["qs_aa"] = sbf("f_qs_aa", [128, 512])
                    s["qs_c0"] = sbf("f_qs_c0", [128, 512])
                    s["qs_c1"] = sbf("f_qs_c1", [128, 512])
                    s["qs_m"] = sbf("f_qs_m", [128, 3, 32])
                    s["qs_d"] = sbf("f_qs_d", [128, 32])
                    s["wqd_sb"] = sbf("wqd_sb", [128, NJT, C], BF16)
                    s["hqT"] = sbf("hqT", [128, NJT, 1024], BF16)
                    s["xqT_S"] = sbf("xqT_S", [128, NCC, 1024], BF16)
                    s["wgb"] = sbf("wgb", [128, 2, NCC, 512], BF16)
                    s["wub"] = sbf("wub", [128, 2, NCC, 512], BF16)
                    s["hg"] = sbf("hg", [128, 512])
                    s["hsb"] = sbf("hsb", [128, 512])
                    s["hq"] = sbf("hq", [128, 2, 512], BF16)
                    s["g_sb"] = sbf("g_sb", [128, C])
                    s["s_ld"] = sbf("s_ld", [128, 2, C], BF16)
                    s["xr"] = sbf("xr", [128, 2, C])
                    s["o_sb"] = sbf("o_sb", [128, 2, C])
                    if "F" in ph:
                        b.emit_f()
                    else:
                        doq = b.streams["So0"]
                        for t in range(b.NT):
                            nc.scalar.wait_ge(b.ve.sem, b.ve.n)
                            doq.dma(g["out_d"][sl(t), :], g["x_d"][sl(t), :])
                        nc.sync.wait_ge(doq.sem, 16 * doq.n)

    return nc


def kernel(**inputs):
    T = 4096
    nc = build_nc(T)
    x = np.ascontiguousarray(np.asarray(inputs["L0_attn_x"], dtype=np.float32))
    weights = {k: np.ascontiguousarray(np.asarray(v, dtype=np.float32))
               for k, v in inputs.items() if k != "L0_attn_x"}
    in_maps = []
    for bb in range(N_CORES):
        m = {"L0_attn_x": x[bb]}
        m.update(weights)
        in_maps.append(m)
    res = run_bass_kernel_spmd(nc, in_maps, core_ids=list(range(N_CORES)))
    return np.stack([res.results[bb]["out"] for bb in range(N_CORES)], axis=0)


# version 18
# speedup vs baseline: 1.0836x; 1.0527x over previous
import math
import numpy as np
import concourse.bass as bass
import concourse.mybir as mybir
from concourse.bass_utils import run_bass_kernel_spmd

F32 = mybir.dt.float32
BF16 = mybir.dt.bfloat16
I32 = mybir.dt.int32
AF = mybir.ActivationFunctionType
ALU = mybir.AluOpType
AX = mybir.AxisListType

C = 1024
J = 2816
NCC = 8
NJT = 22
QSC = float(np.float32(0.125 * math.sqrt(0.125)))
KSC = float(np.float32(math.sqrt(0.125)))
EPS = 1e-5
N_CORES = 8


def sl(i, n=128):
    return slice(i * n, (i + 1) * n)


class Eng:
    def __init__(self, eng, semh, serialize):
        self.e = eng
        self.sem = semh
        self.n = 0
        self.serialize = serialize

    def wait(self, other, val):
        if val > 0:
            self.e.wait_ge(other.sem, val)

    def wait_dma(self, dmaq, n):
        if n > 0:
            self.e.wait_ge(dmaq.sem, 16 * n)

    def op(self, name, *a, **k):
        if self.serialize and self.n > 0:
            self.e.wait_ge(self.sem, self.n)
        r = getattr(self.e, name)(*a, **k)
        r.then_inc(self.sem, 1)
        self.n += 1
        return self.n


class DmaQ:

    def __init__(self, eng):
        self.e = eng

    def wait(self, other, val):
        if val > 0:
            self.e.wait_ge(other.sem, val)

    def wait_dma(self, stream, n):
        if n > 0:
            self.e.wait_ge(stream.sem, 16 * n)


class Stream:

    def __init__(self, q, semh):
        self.q = q
        self.sem = semh
        self.n = 0

    def dma(self, out, in_, **k):
        self.q.e.dma_start(out=out, in_=in_, **k).then_inc(self.sem, 16)
        self.n += 1
        return self.n


class Builder:
    def __init__(self, nc, T):
        self.nc = nc
        self.T = T
        self.NT = T // 128
        self.NSUP = T // 512
        self.NFS = T // 1024
        self._tr_free = [0, 0]
        self._sg_free = 0
        self._wcast_free_pe = [0, 0]
        self._psb0_free_ve = 0
        self._psb0_free_ac = 0
        self._psb1_free_ve = 0
        self._qp_free_ve = 0
        self._qp_free_ac = 0
        self._ao_free = None
        self.aosb_free_ve = [0, 0]
        self._expb_free_pe = [0, 0]
        self._sim_free_ac = [0, 0]
        self._dsw_free = 0
        self._dtsw_free = 0
        self._rb_free = 0
        self._ln_ac_free = 0
        self._ln_ve_free = 0
        self._htr_free = 0
        self._ld_dt = 0
        self.wffn_parts = []
        self._pq_small_free = 0

    def tile_barrier(self):
        gp = self.pe.n
        self.ve.wait(self.pe, gp)
        self.ac.wait(self.pe, gp)

    def quant(self, v_ap, out_ap, F, v_ve=0, v_dma=0, out_gate_pe=0,
              eng=None, pq=False):
        ve, s = self.ve, self.s
        e = eng if eng is not None else ve
        pfx = "pq_" if pq else "qs_"
        nb = F // 16
        aa = s[pfx + "aa"][:, :F]
        cx = s[pfx + "c0"][:, :F]
        cy = s[pfx + "c1"][:, :F]
        mm = s[pfx + "m"][:, 0, :nb]
        sc = s[pfx + "m"][:, 1, :nb]
        rcp = s[pfx + "m"][:, 2, :nb]
        dqs = s[pfx + "d"][:, :nb]
        v3 = v_ap.rearrange("p (b i) -> p b i", i=16)

        if v_dma:
            ve.wait_dma(v_dma[0], v_dma[1])
        if e is not ve:
            ve.wait(e, self._pq_small_free)
        ve.op("tensor_reduce", mm, v3, AX.X, ALU.max,
              apply_absolute_value=True)
        ve.op("tensor_scalar", sc, mm, s["rcp6"][:, 0:1], 1e-12,
              ALU.mult, ALU.max)
        ve.op("reciprocal", rcp, sc)
        nved = ve.op("tensor_copy", dqs, sc)
        if e is not ve:
            e.wait(ve, nved)
            if v_ve:
                e.wait(ve, v_ve)
        e.op("tensor_tensor", cx.rearrange("p (b i) -> p b i", i=16), v3,
             rcp.unsqueeze(2).broadcast_to([128, nb, 16]), ALU.mult)
        if e is ve:
            e.op("tensor_scalar", aa.bitcast(I32), cx.bitcast(I32),
                 s["imask"][:, 0:1], None, ALU.bitwise_and)
        else:
            e.op("tensor_scalar", aa.bitcast(I32), cx.bitcast(I32),
                 -2147483648.0, None, ALU.bitwise_xor)
            e.op("tensor_tensor", aa, aa, cx, ALU.max)
        e.op("tensor_scalar", cy, aa, 4194305.0, None, ALU.mult)
        e.op("tensor_tensor", cx, cy, aa, ALU.subtract)
        e.op("tensor_tensor", cy, cy, cx, ALU.subtract)
        e.op("scalar_tensor_tensor", cy, aa, 0.75, cy, ALU.is_gt, ALU.mult)
        e.op("tensor_scalar", cx, aa, 0.25, None, ALU.is_gt)
        e.op("scalar_tensor_tensor", cx, aa, 0.75, cx, ALU.is_gt, ALU.add)
        e.op("scalar_tensor_tensor", cy, cx, 0.5, cy, ALU.mult, ALU.max)
        smk = (s["smask"][:, 0:1] if e is ve else -2147483648.0)
        e.op("scalar_tensor_tensor", cx.bitcast(I32), v_ap.bitcast(I32),
             smk, cy.bitcast(I32), ALU.bitwise_and,
             ALU.bitwise_or)
        if out_gate_pe:
            e.wait(self.pe, out_gate_pe)
        n = e.op("tensor_tensor",
                 out_ap.rearrange("p (b i) -> p b i", i=16),
                 cx.rearrange("p (b i) -> p b i", i=16),
                 dqs.unsqueeze(2).broadcast_to([128, nb, 16]), ALU.mult)
        if e is not ve:
            self._pq_small_free = n
        return n

    def ln_apply(self, v_ap, out_ap, F):
        ve, ac, s = self.ve, self.ac, self.s
        ng = (F + 511) // 512
        ve.wait(ac, self._ln_ac_free)
        for gi in range(ng):
            ve.op("bn_stats", s["bn6"][:, gi, :], v_ap[:, sl(gi, 512)])
        nv = ve.op("bn_aggr", s["st2"][:], s["bn6"][:, :ng, :])
        ac.wait(ve, nv)
        self._ln_ac_free = ac.op("activation", s["lnv"][:], s["st2"][:, 1:2],
                                 AF.Ln, bias=s["epsT"][:, 0:1])
        ac.wait(ve, self._ln_ve_free)
        na = ac.op("activation", s["rstd"][:], s["lnv"][:], AF.Exp, scale=-0.5)
        ve.wait(ac, na)
        napp = ve.op("tensor_scalar", out_ap, v_ap, s["st2"][:, 0:1],
                     s["rstd"][:], ALU.subtract, ALU.mult)
        self._ln_ve_free = napp
        return napp

    def transpose_chunks(self, src_ap, nch, dst_fn, pe_gate_ve,
                         gate_ac=None, gate_po=None):
        pe, ve, s = self.pe, self.ve, self.s
        pe.wait(ve, pe_gate_ve)
        if gate_ac is not None:
            pe.wait(self.ac, gate_ac)
        if gate_po is not None:
            pe.wait(self.po, gate_po)
        trA = s["ps_o"][:, 3, 0:64].bitcast(BF16)
        trB = s["ps_o"][:, 2, 0:64].bitcast(BF16)
        for ch in range(nch):
            tgt = trA if ch % 2 == 0 else trB
            pe.wait(ve, self._tr_free[ch % 2])
            npt = pe.op("transpose", tgt[:, 0:128], src_ap[:, sl(ch)],
                        s["identb"][:])
            ve.wait(pe, npt)
            self._tr_free[ch % 2] = ve.op("tensor_copy", dst_fn(ch),
                                          tgt[:, 0:128])
        return ve.n, pe.n

    def emit_w(self):
        pe, ve, ac, po, ld, stq = (self.pe, self.ve, self.ac, self.po,
                                   self.ld, self.stq)
        s, g, st_ = self.s, self.g, self.streams

        po.op("memset", s["epsT"][:], EPS)
        po.op("memset", s["tiny1"][:], 1e-30)
        po.op("memset", s["imask"][:, 0:1], 0x7FFFFFFF)
        po.op("memset", s["smask"][:, 0:1], -2147483648)
        po.op("memset", s["identb"][:], 0.0)
        po.op("affine_select", out=s["identb"][:], in_=s["identb"][:],
              compare_op=ALU.not_equal, fill=1.0, base=0,
              pattern=[[-1, 128]], channel_multiplier=1)
        po.op("memset", s["triL"][:], 1.0)
        po.op("affine_select", out=s["triL"][:], in_=s["triL"][:],
              compare_op=ALU.is_ge, fill=0.0, base=0,
              pattern=[[1, 128]], channel_multiplier=-1)
        po.op("memset", s["negb"][:], -1e30)
        for j in range(4):
            po.op("affine_select", out=s["negb"][:, j, :],
                  in_=s["negb"][:, j, :], compare_op=ALU.is_ge, fill=0.0,
                  base=128 * j - 1, pattern=[[-1, 512]], channel_multiplier=1)
        po.op("memset", s["VhX"][:, :, 64:128], 1.0)
        po.op("memset", s["KT"][64:65, :], 1.0)
        po.op("memset", s["ones64"][:], 1.0)
        po.op("memset", s["ones128"][:], 1.0)
        po.op("memset", s["k2r"][:], 0.0)
        po.op("memset", s["rcp6"][:, 1:2], 6.0)
        n_po_w = po.n
        ve.wait(po, n_po_w)
        ve.op("reciprocal", s["rcp6"][:, 0:1], s["rcp6"][:, 1:2])

        for i, t in enumerate([g["lng_d"], g["lnb_d"], g["lnog_d"],
                               g["lnob_d"]]):
            st_["Lmisc"].dma(s["greps"][:, i, :],
                             t.unsqueeze(0).broadcast_to([128, C]))
        self._n_greps = st_["Lmisc"].n

        ve.wait(po, n_po_w)
        ac.wait(po, n_po_w)
        pe.wait(po, n_po_w)
        self._wtile_free = 0

        def cast_transpose(src_dram, row_tiles, dst, scale_rows=None):
            for rt in range(row_tiles):
                ld.wait(ve, self._wtile_free)
                nl = st_["Lwt"].dma(s["wtile"][:, :C], src_dram[sl(rt), :])
                ve.wait_dma(st_["Lwt"], nl)
                if scale_rows is None:
                    ncst = ve.op("tensor_copy", s["wcast"][:, 0, :C],
                                 s["wtile"][:, :C])
                else:
                    lo, hi, sc_ = scale_rows
                    ncst = ve.op("tensor_scalar", s["wcast"][lo:hi, 0, :C],
                                 s["wtile"][lo:hi, :C], sc_, None, ALU.mult)
                    if hi < 128:
                        ncst = ve.op("tensor_copy", s["wcast"][hi:128, 0, :C],
                                     s["wtile"][hi:128, :C])
                self._wtile_free = ncst
                _, npe0 = self.transpose_chunks(
                    s["wcast"][:, 0, :C], NCC, lambda cc: dst[:, cc, sl(rt)],
                    ncst)
                self._wcast_free_pe[0] = npe0

        cast_transpose(g["wq_d"], NCC, s["w_qT"], scale_rows=(0, 128, QSC))
        cast_transpose(g["wkv_d"], 1, s["w_kvT"], scale_rows=(0, 64, KSC))
        cast_transpose(g["wo_d"], NCC, s["w_oT"])

        self._wstage_free = 0
        self._wtile_free_po = 0

        def w_tile_ve(src, rt, rowlen, dstT):
            wslot = self._wslot
            self._wslot ^= 1
            ld.wait(ve, self._wtile_free_po)
            nl = st_["Lwt"].dma(s["wtile"][:, :rowlen], src[sl(rt), :])
            ve.wait(pe, self._wcast_free_pe[wslot])
            for off in range(0, rowlen, 1024):
                w = min(1024, rowlen - off)
                self.quant(s["wtile"][:, off:off + w],
                           s["wcast"][:, wslot, off:off + w], w,
                           v_dma=(st_["Lwt"], nl))
            self._wtile_free_po = ve.n
            return (rt, rowlen, dstT, wslot, ve.n)

        def w_tile_pe(pend):
            rt, rowlen, dstT, wslot, nq = pend
            nch = rowlen // 128
            for glo in range(0, nch, 8):
                ghi = min(glo + 8, nch)
                ve.wait_dma(st_["Swsp"], self._wstage_free)
                nev, npe = self.transpose_chunks(
                    s["wcast"][:, wslot, glo * 128:ghi * 128], ghi - glo,
                    lambda ch: s["xqTst"][:, 2, ch, :], nq)
                self._wcast_free_pe[wslot] = npe
                stq.wait(ve, nev)
                self._wstage_free = st_["Swsp"].dma(
                    dstT[:, glo:ghi, sl(rt)],
                    s["xqTst"][:, 2, 0:ghi - glo, :])

        self._w_tile_ve = w_tile_ve
        self._w_tile_pe = w_tile_pe
        self._w_pend = None
        self._wslot = 0
        self.wffn_parts = (
            [(g["wg_d"], rt, C, g["wqgT_s"]) for rt in range(NJT)] +
            [(g["wu_d"], rt, C, g["wquT_s"]) for rt in range(NJT)] +
            [(g["wd_d"], rt, J, g["wqdT_s"]) for rt in range(NCC)])

    def p0_ve_part(self, t):
        ve, ld = self.ve, self.ld
        s, g, st_ = self.s, self.g, self.streams
        slot = t % 2
        ld.wait(ve, self.xt_free[slot])
        nx = st_["Lx%d" % slot].dma(s["x_t"][:, slot, :], g["x_d"][sl(t), :])
        ve.wait_dma(st_["Lx%d" % slot], nx)
        nxn = self.ln_apply(s["x_t"][:, slot, :], s["xn_t"][:], C)
        self.xt_free[slot] = nxn
        self.quant(s["xn_t"][:], s["xq_t"][:, slot, :], C, v_ve=nxn,
                   out_gate_pe=self._xqt_pe[slot])
        self.ln_apply(s["xn_t"][:], s["t0f"][:], C)
        ve.op("tensor_tensor", s["t0f"][:], s["t0f"][:],
              s["greps"][:, 0, :], ALU.mult)
        ve.wait(self.pe, self._xnlt_pe[slot])
        return ve.op("tensor_tensor", s["xnl_t"][:, slot, :], s["t0f"][:],
                     s["greps"][:, 1, :], ALU.add)

    def p0_pe_part(self, t, nxnl):
        pe, ve, ac, ld, stq = self.pe, self.ve, self.ac, self.ld, self.stq
        s, g, st_ = self.s, self.g, self.streams
        ps_o = s["ps_o"]
        slot = t % 2
        tt = t % 4
        nq = nxnl
        ve.wait_dma(st_["Sxst%d" % slot], self.xst_free[slot])
        nev, self._xqt_pe[slot] = self.transpose_chunks(
            s["xq_t"][:, slot, :], NCC,
            lambda cc, _b=slot: s["xqTst"][:, _b, cc, :], nq)
        stq.wait(ve, nev)
        self.xst_free[slot] = st_["Sxst%d" % slot].dma(
            g["xqT_s"][:, :, sl(t)], s["xqTst"][:, slot, :, :])
        nevl, self._xnlt_pe[slot] = self.transpose_chunks(
            s["xnl_t"][:, slot, :], NCC,
            lambda cc, _tt=tt: s["xnlT"][:, cc, sl(_tt)], nxnl)
        kvp = ps_o[:, 2, 128:256]
        pe.wait(ve, nevl)
        pe.wait(ve, self._tr_free[1])
        for cc in range(NCC):
            nkv = pe.op("matmul", kvp, s["w_kvT"][:, cc, :],
                        s["xnlT"][:, cc, sl(tt)],
                        start=(cc == 0), stop=(cc == NCC - 1))
        ac.wait(pe, nkv)
        nktc = ac.op("activation", s["KT"][0:64, sl(t)], kvp[0:64, :],
                     AF.Copy)
        nvt = ac.op("activation", s["vtmp"][:], kvp[64:128, :], AF.Copy)
        ac.wait(pe, self._ksq_free_pe)
        nks = ac.op("activation", s["ksq"][:], s["KT"][0:64, sl(t)],
                    AF.Square)
        k2p = ps_o[0:1, 2, 256:384]
        pe.wait(ac, nks)
        pe.wait(ve, self._k2_evac)
        nk2m = pe.op("matmul", k2p, s["ones64"][:, 0:1], s["ksq"][:],
                     start=True, stop=True)
        self._ksq_free_pe = nk2m
        ve.wait(pe, nk2m)
        ve.op("reduce_max", s["smalls"][0:1, 0:1], k2p, AX.X)
        ve.wait(ac, self._k2r_ac)
        self._k2_evac = ve.op(
            "tensor_scalar", s["k2r"][0:1, 0:1], s["k2r"][0:1, 0:1],
            s["smalls"][0:1, 0:1], None, ALU.max)
        self.n_k2 = self._k2_evac
        vtr = ps_o[:, 2, 384:448].bitcast(BF16)
        pe.wait(ve, self._k2_evac)
        npv = pe.op("transpose", vtr[:, 0:64], s["vtmp"][:],
                    s["identb"][0:64, 0:64])
        ve.wait(pe, npv)
        ve.op("tensor_copy", s["VhX"][:, t, 0:64], vtr[:, 0:64])
        self._tr_free[1] = ve.op("tensor_copy", s["VhX"][:, t, 128:192],
                                 vtr[:, 0:64])

    def emit_qhat(self, sup):
        pe, ve, ac, ld, stq = self.pe, self.ve, self.ac, self.ld, self.stq
        s, g, st_ = self.s, self.g, self.streams
        ps_a, ps_b = s["ps_a"], s["ps_b"]
        ac.wait(ve, self.n_k2)
        nlnk2 = ac.op("activation", s["k2r"][0:1, 1:2],
                      s["k2r"][0:1, 0:1], AF.Ln,
                      bias=s["tiny1"][0:1, 0:1])
        self._k2r_ac = nlnk2
        qn_evac = 0
        pe.wait(ac, self._sim_free_ac[0])
        pe.wait(ac, self._sim_free_ac[1])
        for dd in range(NCC):
            qp = ps_a[:, 0, :]
            pe.wait(ac, self._qp_free_ve)
            for cc in range(NCC):
                nqm = pe.op("matmul", qp, s["w_qT"][:, cc, sl(dd)],
                            s["xnlT"][:, cc, :],
                            start=(cc == 0), stop=(cc == NCC - 1))
            ac.wait(pe, nqm)
            ac.op("activation", s["QT"][0:64, 2 * dd, :], qp[0:64, :],
                  AF.Copy)
            nqe = ac.op("activation", s["QT"][0:64, 2 * dd + 1, :],
                        qp[64:128, :], AF.Copy)
            self._qp_free_ve = nqe
            ac.wait(pe, self._qsq_free_pe)
            nqs = ac.op(
                "activation", s["qsq"][:].rearrange("p (h q) -> p h q",
                                                    q=512),
                s["QT"][0:64, 2 * dd:2 * dd + 2, :], AF.Square)
            qnp = ps_b[0:1, 0, :]
            pe.wait(ac, nqs)
            pe.wait(ve, self._psb0_free_ve)
            pe.op("matmul", qnp, s["ones64"][:, 0:1],
                  s["qsq"][:, 0:512], start=True, stop=False)
            nqn = pe.op("matmul", qnp, s["ones64"][:, 0:1],
                        s["qsq"][:, 512:1024], start=False, stop=True)
            self._qsq_free_pe = nqn
            ve.wait(pe, nqn)
            if dd == 0:
                ve.wait_dma(st_["Smr"], self.nmd)
            qn_evac = ve.op("tensor_copy", s["qn"][0:1, dd, :], qnp)
            self._psb0_free_ve = qn_evac
        qf = s["qn"][0:1, :, :].rearrange("a b c -> a (b c)")
        ac.wait(ve, qn_evac)
        nmr1 = ac.op("activation", qf, qf, AF.Ln,
                     bias=s["tiny1"][0:1, 0:1])
        ve.wait(ac, max(nmr1, nlnk2))
        nmr2 = ve.op("tensor_scalar", qf, qf, s["k2r"][0:1, 1:2], None,
                     ALU.add)
        ac.wait(ve, nmr2)
        nmr3 = ac.op("activation", qf, qf, AF.Exp, scale=0.5)
        ve.wait(ac, nmr3)
        nmrb = ve.op("tensor_scalar", qf, qf, -1.0, None, ALU.mult)
        stq.wait(ve, nmrb)
        stq.wait_dma(st_["Lmr"], self.nmq)
        self.nmd = st_["Smr"].dma(g["mrow_d"].unsqueeze(0),
                                  s["qn"][0:1, :, :])
        ld.wait_dma(st_["Smr"], self.nmd)
        ld.wait(pe, self._pe_after_sweeps)
        self.nmq = st_["Lmr"].dma(
            s["QT"][64:65, :, :].rearrange("o (p h) q -> o p h q", h=2),
            g["mrow_d"].unsqueeze(1).broadcast_to([NCC, 2, 512])
            .unsqueeze(0))
        self._preA_ve = ve.n

    def emit_sweeps(self, sup, inject=None):
        pe, ve, ac, ld, stq = self.pe, self.ve, self.ac, self.ld, self.stq
        po = self.po
        s, g, st_ = self.s, self.g, self.streams
        ps_a, ps_b, ps_o = s["ps_a"], s["ps_b"], s["ps_o"]
        kvmax = 4 * (sup + 1)
        pe.wait_dma(st_["Lmr"], self.nmq)
        sv_ve = self._preA_ve
        for w in range(NCC):
            if inject is not None and w < len(inject) and inject[w]:
                inject[w]()
            h0, h1 = 2 * w, 2 * w + 1
            ao0 = ps_o[:, 0, :]
            ao1 = ps_o[:, 1, :]
            npvm = 0
            pend = None

            def emit_pv(k, nex, pv_gate_ve):
                pe.wait(ac, nex)
                if pv_gate_ve:
                    pe.wait(ve, pv_gate_ve)
                if k == 0:
                    _e, _n = self._ao_free
                    pe.wait(_e, _n)
                ebuf = s["expb"][:, k % 2, :, :]
                pe.op("matmul", ao0, s["VhX"][:, k, 0:128],
                      ebuf[:, 0, :], start=(k == 0),
                      stop=(k == kvmax - 1), skip_group_check=True)
                n = pe.op("matmul", ao1, s["VhX"][:, k, 64:192],
                          ebuf[:, 1, :], start=(k == 0),
                          stop=(k == kvmax - 1), skip_group_check=True)
                self._expb_free_pe[k % 2] = n
                return n

            for k in range(kvmax):
                simp = ps_a if k % 2 == 0 else ps_b
                pe.wait(ac, self._sim_free_ac[k % 2])
                if w == 0 and k < 2:
                    pe.wait(ve, sv_ve)
                diag = k >= 4 * sup
                jj = k - 4 * sup
                pe.op("matmul", simp[:, 0, :], s["KT"][:, sl(k)],
                      s["QT"][:, h0, :], start=True, stop=not diag)
                if diag:
                    pe.op("matmul", simp[:, 0, :], s["triL"][:],
                          s["negb"][:, jj, :], start=False, stop=True)
                pe.op("matmul", simp[:, 1, :], s["KT"][:, sl(k)],
                      s["QT"][:, h1, :], start=True, stop=not diag)
                if diag:
                    pe.op("matmul", simp[:, 1, :], s["triL"][:],
                          s["negb"][:, jj, :], start=False, stop=True)
                nqk = pe.n
                if pend is not None:
                    npvm = emit_pv(*pend)
                ac.wait(pe, nqk)
                ac.wait(pe, self._expb_free_pe[k % 2])
                ebuf = s["expb"][:, k % 2, :, :]
                nex = ac.op("activation",
                            ebuf.rearrange("p h q -> p (h q)"),
                            simp[:, :, :].rearrange("p h q -> p (h q)"),
                            AF.Exp)
                self._sim_free_ac[k % 2] = nex
                pend = (k, nex, 0)
            npvm = emit_pv(*pend)
            wsl = w % 2
            aob = s["aosb"][:, wsl, :, :]
            ac.wait(pe, npvm)
            ac.wait(ve, self.aosb_free_ve[wsl])
            ac.wait_dma(st_["Sdsw"], self._dsw_free)
            ac.op("activation", aob[:, 0, :], ao0, AF.Copy)
            naosb = ac.op("activation", aob[:, 1, :], ao1, AF.Copy)
            self._ao_free = (ac, naosb)
            ac.op("activation", s["rb"][64:128, :], aob[64:128, 0, :],
                  AF.Ln)
            ac.op("activation", s["rb"][0:64, :], aob[0:64, 1, :], AF.Ln)
            nrp = ac.op("activation", s["rb"][:], s["rb"][:], AF.Exp,
                        scale=-1.0)
            stq.wait(ac, nrp)
            stq.wait_dma(st_["Ldt"], self._ld_dt)
            st_["Sdsw"].dma(g["dsw_d"][64:128, :], s["rb"][64:128, :])
            ndw = st_["Sdsw"].dma(g["dsw_d"][0:64, :], s["rb"][0:64, :])
            self._dsw_free = ndw
            ld.wait_dma(st_["Sdsw"], ndw)
            ld.wait(ve, self._dtsw_free)
            st_["Ldt"].dma(s["dtsw"][0:64, :], g["dsw_d"][64:128, :])
            ndr = st_["Ldt"].dma(s["dtsw"][64:128, :],
                                 g["dsw_d"][0:64, :])
            self._ld_dt = ndr
            ve.wait_dma(st_["Ldt"], ndr)
            ve.wait(ac, naosb)
            ve.op("tensor_tensor", s["AOn"][0:64, w, :],
                  aob[0:64, 0, :], s["dtsw"][0:64, :], ALU.mult)
            naon = ve.op("tensor_tensor", s["AOn"][64:128, w, :],
                         aob[64:128, 1, :], s["dtsw"][64:128, :],
                         ALU.mult)
            self._dtsw_free = naon
            self.aosb_free_ve[wsl] = naon

        aon_all = ve.n
        for sub in range(4):
            gt = sup * 4 + sub
            op0 = ps_o[:, 0, :]
            op1 = ps_o[:, 1, :]
            pe.wait(ve, aon_all)
            pe.wait(ve, self._op_free_ve)
            for ee in range(NCC):
                pe.op("matmul", op0, s["AOn"][:, ee, sl(sub)],
                      s["w_oT"][:, ee, 0:512],
                      start=(ee == 0), stop=(ee == NCC - 1),
                      skip_group_check=True)
                nop = pe.op("matmul", op1, s["AOn"][:, ee, sl(sub)],
                            s["w_oT"][:, ee, 512:1024],
                            start=(ee == 0), stop=(ee == NCC - 1),
                            skip_group_check=True)
            ve.wait(pe, nop)
            opv = ps_o[:, 0:2, :].rearrange("p a b -> p (a b)")
            self.ln_apply(opv, s["t0f"][:], C)
            ve.op("tensor_tensor", s["t0f"][:], s["t0f"][:],
                  s["greps"][:, 2, :], ALU.mult)
            ve.wait_dma(st_["Ss%d" % (sub % 2)], self.sst_free[sub % 2])
            ns = ve.op("tensor_tensor", s["st_s"][:, sub % 2, :],
                       s["t0f"][:], s["greps"][:, 3, :], ALU.add)
            self._op_free_ve = ns
            stq.wait(ve, ns)
            self.sst_free[sub % 2] = st_["Ss%d" % (sub % 2)].dma(
                g["s_s"][sl(gt), :], s["st_s"][:, sub % 2, :])
        self._ao_free = (ve, self._op_free_ve)
        self._pe_after_sweeps = pe.n

    def emit_a(self):
        pe, ve, ac, po, ld, stq = (self.pe, self.ve, self.ac, self.po,
                                   self.ld, self.stq)
        s, st_ = self.s, self.streams
        NSUP = self.NSUP

        ve.wait_dma(st_["Lmisc"], self._n_greps)
        ve.wait_dma(st_["Swsp"], st_["Swsp"].n)
        self.xt_free = [0, 0]
        self.xst_free = [0, 0]
        self.sst_free = [0, 0]
        self._vtr_free = 0
        self._k2_evac = 0
        self._k2r_ac = 0
        self._op_free_ve = 0
        self._xqt_pe = [0, 0]
        self._xnlt_pe = [0, 0]
        self._pe_after_sweeps = 0
        self._preA_ve = 0
        self._ao_free = (ve, 0)
        self.aosb_free_ve = [0, 0]
        self.nmd = 0
        self.nmq = 0
        self.n_k2 = 0

        for sup in range(NSUP + 1):
            if sup == 0:
                for t in range(4):
                    nx = self.p0_ve_part(t)
                    self.p0_pe_part(t, nx)
            if sup >= 1:
                base = 4 * sup
                nx_hold = {}

                def mk_both(ttv, ttp, _b=base, _h=nx_hold, _last=(sup == NSUP)):
                    def f():
                        if not _last:
                            if ttv is not None:
                                _h[ttv] = self.p0_ve_part(_b + ttv)
                            if ttp is not None:
                                self.p0_pe_part(_b + ttp, _h[ttp])
                        if self._w_pend is not None:
                            self._w_tile_pe(self._w_pend)
                            self._w_pend = None
                        if self.wffn_parts:
                            self._w_pend = self._w_tile_ve(
                                *self.wffn_parts.pop(0))
                    return f

                inject = [mk_both(0, None), mk_both(1, 0),
                          mk_both(2, 1), mk_both(3, 2),
                          mk_both(None, 3), mk_both(None, None),
                          mk_both(None, None), mk_both(None, None)]
                self.emit_sweeps(sup - 1, inject)
            if sup < NSUP:
                self.emit_qhat(sup)
        while self.wffn_parts or self._w_pend is not None:
            if self._w_pend is not None:
                self._w_tile_pe(self._w_pend)
                self._w_pend = None
            if self.wffn_parts:
                self._w_pend = self._w_tile_ve(*self.wffn_parts.pop(0))

    def emit_f(self):
        pe, ve, ac, po, ld, stq = (self.pe, self.ve, self.ac, self.po,
                                   self.ld, self.stq)
        s, g, st_ = self.s, self.g, self.streams
        ps_a, ps_b, ps_o = s["ps_a"], s["ps_b"], s["ps_o"]
        NFS = self.NFS

        bar_pe, bar_ve, bar_ac = pe.n, ve.n, ac.n
        ld.wait(pe, bar_pe)
        ld.wait(ve, bar_ve)
        ld.wait(ac, bar_ac)
        for nm in ["Swsp", "Sxst0", "Sxst1", "Ss0", "Ss1", "Smr", "Sdsw"]:
            ld.wait_dma(st_[nm], st_[nm].n)
        pe.wait(ve, bar_ve)
        pe.wait(ac, bar_ac)
        ve.wait(pe, bar_pe)
        ac.wait(pe, bar_pe)

        nwd = st_["Lmisc"].dma(s["wqd_sb"][:], g["wqdT_s"][:, :, :])
        hq_read_pe = 0
        xq_read_pe = 0
        wg_free = [0, 0]
        xr_free = [0, 0]
        sld_free = [0, 0]
        o_free = [0, 0]
        g_done = 0

        hq_pe = [0, 0]
        for S in range(NFS):
            ld.wait(pe, xq_read_pe)
            nxq = st_["Lxqs"].dma(s["xqT_S"][:], g["xqT_s"][:, :, sl(S, 1024)])
            silu_done = 0
            tt_done = 0
            pend = None
            first_evac = True
            ui = 0
            for jc in range(6):
                jw = 512 if jc < 5 else 256
                jb = jc % 2
                ld.wait(pe, wg_free[jb])
                st_["Lwg%d" % jb].dma(
                    s["wgb"][:, jb, :, :jw],
                    g["wqgT_s"][:, :, jc * 512:jc * 512 + jw])
                nwu = st_["Lwg%d" % jb].dma(
                    s["wub"][:, jb, :, :jw],
                    g["wquT_s"][:, :, jc * 512:jc * 512 + jw])
                pe.wait_dma(st_["Lwg%d" % jb], nwu)
                if jc == 0:
                    pe.wait_dma(st_["Lxqs"], nxq)
                for sub in range(8):
                    slot = ui % 2
                    gp = ps_a[:, 0, :jw]
                    up = ps_b[:, 0, :jw]
                    pe.wait(ac, silu_done)
                    pe.wait(ve, tt_done)
                    for cc in range(NCC):
                        pe.op("matmul", gp, s["xqT_S"][:, cc, sl(sub)],
                              s["wgb"][:, jb, cc, :jw],
                              start=(cc == 0), stop=(cc == NCC - 1),
                              skip_group_check=True)
                        ngu = pe.op("matmul", up, s["xqT_S"][:, cc, sl(sub)],
                                    s["wub"][:, jb, cc, :jw],
                                    start=(cc == 0), stop=(cc == NCC - 1),
                                    skip_group_check=True)
                    wg_free[jb] = ngu
                    xq_read_pe = ngu
                    npt_prev = 0
                    if pend is not None:
                        pslot, pjw, pnq, pdst = pend
                        trC = ps_a[:, 1, 0:256].bitcast(BF16)
                        pe.wait(ve, pnq)
                        pe.wait(ac, self._htr_free)
                        for ji in range(pjw // 128):
                            npt_prev = pe.op("transpose", trC[:, sl(ji)],
                                             s["hq"][:, pslot, sl(ji)],
                                             s["identb"][:])
                        hq_pe[pslot] = npt_prev
                    ac.wait(pe, ngu)
                    ac.wait(ve, tt_done)
                    silu_done = ac.op("activation", s["hg"][:, :jw], gp,
                                      AF.Silu)
                    ve.wait(ac, silu_done)
                    ntt = ve.op("tensor_tensor", s["hsb"][:, :jw],
                                s["hg"][:, :jw], up, ALU.mult)
                    tt_done = ntt
                    nq = self.quant(s["hsb"][:, :jw], s["hq"][:, slot, :jw],
                                    jw, v_ve=ntt, out_gate_pe=hq_pe[slot])
                    if pend is not None:
                        pslot, pjw, pnq, pdst = pend
                        trC = ps_a[:, 1, 0:256].bitcast(BF16)
                        if first_evac:
                            ac.wait(pe, hq_read_pe)
                            first_evac = False
                        ac.wait(pe, npt_prev)
                        self._htr_free = ac.op(
                            "activation", pdst,
                            trC[:, 0:pjw].rearrange("p (a b) -> p a b",
                                                    b=128), AF.Copy)
                    pend = (slot, jw,  nq,
                            s["hqT"][:, jc * 4:jc * 4 + jw // 128, sl(sub)])
                    ui += 1
            pslot, pjw, pnq, pdst = pend
            trC = ps_a[:, 1, 0:256].bitcast(BF16)
            pe.wait(ve, pnq)
            pe.wait(ac, self._htr_free)
            for ji in range(pjw // 128):
                npt = pe.op("transpose", trC[:, sl(ji)],
                            s["hq"][:, pslot, sl(ji)], s["identb"][:])
            hq_pe[pslot] = npt
            if first_evac:
                ac.wait(pe, hq_read_pe)
            ac.wait(pe, npt)
            self._htr_free = ac.op(
                "activation", pdst,
                trC[:, 0:pjw].rearrange("p (a b) -> p a b", b=128), AF.Copy)
            nhq = ac.n
            pe.wait(ac, nhq)
            pe.wait_dma(st_["Lmisc"], nwd)
            for sub in range(8):
                gt = S * 8 + sub
                fb = sub % 2
                ld.wait(po, xr_free[fb])
                ld.wait(po, sld_free[fb])
                st_["Lxr%d" % fb].dma(s["xr"][:, fb, :], g["x_d"][sl(gt), :])
                nsl = st_["Lxr%d" % fb].dma(s["s_ld"][:, fb, :],
                                            g["s_s"][sl(gt), :])
                fp0 = ps_o[:, 0, :]
                fp1 = ps_o[:, 1, :]
                pe.wait(ac, g_done)
                for jcc in range(NJT):
                    pe.op("matmul", fp0, s["hqT"][:, jcc, sl(sub)],
                          s["wqd_sb"][:, jcc, 0:512],
                          start=(jcc == 0), stop=(jcc == NJT - 1),
                          skip_group_check=True)
                    nfm = pe.op("matmul", fp1, s["hqT"][:, jcc, sl(sub)],
                                s["wqd_sb"][:, jcc, 512:1024],
                                start=(jcc == 0), stop=(jcc == NJT - 1),
                                skip_group_check=True)
                hq_read_pe = nfm
                ac.wait(pe, nfm)
                ac.wait(po, sld_free[1 - fb])
                g_done = ac.op(
                    "activation", s["g_sb"][:],
                    ps_o[:, 0:2, :].rearrange("p a b -> p (a b)"), AF.Silu)
                po = self.po
                po.wait(ac, g_done)
                po.wait_dma(st_["Lxr%d" % fb], nsl)
                po.op("tensor_copy", s["t0f"][:], s["s_ld"][:, fb, :])
                po.op("tensor_tensor", s["t0f"][:], s["t0f"][:], s["g_sb"][:],
                      ALU.mult)
                sld_free[fb] = po.n
                po.wait_dma(st_["So%d" % fb], o_free[fb])
                no = po.op("tensor_tensor", s["o_sb"][:, fb, :], s["t0f"][:],
                           s["xr"][:, fb, :], ALU.add)
                xr_free[fb] = no
                stq.wait(po, no)
                o_free[fb] = st_["So%d" % fb].dma(g["out_d"][sl(gt), :],
                                                  s["o_sb"][:, fb, :])
        self.nc.sync.wait_ge(st_["So0"].sem, 16 * st_["So0"].n)
        self.nc.sync.wait_ge(st_["So1"].sem, 16 * st_["So1"].n)


def build_nc(T, phases="WAF"):
    nc = bass.Bass("TRN2", target_bir_lowering=False)
    b = Builder(nc, T)
    b.phases = phases[:3] if phases[:1] else phases
    if phases.endswith("n"):
        b.nointer = True
        phases = phases[:-1]
    if len(phases) > 2 and phases[2:3].isdigit():
        b.astage = int(phases[2])
        b.phases = "WA"
        if "t" in phases:
            b.tstage = int(phases.split("t")[1])
    else:
        b.phases = phases
    NT = b.NT

    g = {}
    g["x_d"] = nc.dram_tensor("L0_attn_x", [T, C], F32, kind="ExternalInput").ap()
    g["wq_d"] = nc.dram_tensor("L0_attn_w_q", [C, C], F32, kind="ExternalInput").ap()
    g["wkv_d"] = nc.dram_tensor("L0_attn_w_kv", [128, C], F32, kind="ExternalInput").ap()
    g["wo_d"] = nc.dram_tensor("L0_attn_w_out", [C, C], F32, kind="ExternalInput").ap()
    g["lng_d"] = nc.dram_tensor("L0_attn_ln_g", [C], F32, kind="ExternalInput").ap()
    g["lnb_d"] = nc.dram_tensor("L0_attn_ln_b", [C], F32, kind="ExternalInput").ap()
    g["lnog_d"] = nc.dram_tensor("L0_attn_lno_g", [C], F32, kind="ExternalInput").ap()
    g["lnob_d"] = nc.dram_tensor("L0_attn_lno_b", [C], F32, kind="ExternalInput").ap()
    g["wg_d"] = nc.dram_tensor("L0_ffn_w_gate", [J, C], F32, kind="ExternalInput").ap()
    g["wu_d"] = nc.dram_tensor("L0_ffn_w_up", [J, C], F32, kind="ExternalInput").ap()
    g["wd_d"] = nc.dram_tensor("L0_ffn_w_down", [C, J], F32, kind="ExternalInput").ap()
    g["out_d"] = nc.dram_tensor("out", [T, C], F32, kind="ExternalOutput").ap()
    g["wqgT_s"] = nc.dram_tensor("wqgT_s", [128, NCC, J], BF16, kind="Internal").ap()
    g["wquT_s"] = nc.dram_tensor("wquT_s", [128, NCC, J], BF16, kind="Internal").ap()
    g["wqdT_s"] = nc.dram_tensor("wqdT_s", [128, NJT, C], BF16, kind="Internal").ap()
    g["xqT_s"] = nc.dram_tensor("xqT_s", [128, NCC, T], BF16, kind="Internal").ap()
    g["s_s"] = nc.dram_tensor("s_s", [T, C], BF16, kind="Internal").ap()
    g["mrow_d"] = nc.dram_tensor("mrow_d", [NCC, 512], BF16, kind="Internal").ap()
    g["dsw_d"] = nc.dram_tensor("dsw_d", [128, 512], F32, kind="Internal").ap()

    from contextlib import ExitStack
    with ExitStack() as st:
        def sb(name, shape, dt=F32):
            return st.enter_context(nc.sbuf_tensor(name, shape, dt))

        def psum(name, shape, dt=F32):
            return st.enter_context(nc.psum_tensor(name, shape, dt))

        s = {}
        s["epsT"] = sb("epsT", [128, 1])
        s["tiny1"] = sb("tiny1", [1, 1])
        s["imask"] = sb("imask", [128, 1], I32)
        s["smask"] = sb("smask", [128, 1], I32)
        s["identb"] = sb("identb", [128, 128], BF16)
        s["triL"] = sb("triL", [128, 128], BF16)
        s["negb"] = sb("negb", [128, 4, 512], BF16)
        s["ones64"] = sb("ones64", [64, 1], BF16)
        s["ones128"] = sb("ones128", [128, 1], BF16)
        s["smalls"] = sb("smalls", [1, 8])
        s["rcp6"] = sb("rcp6", [128, 2])
        s["k2r"] = sb("k2r", [1, 2])
        s["bn6"] = sb("bn6", [128, 2, 6])
        s["st2"] = sb("st2", [128, 2])
        s["lnv"] = sb("lnv", [128, 1])
        s["rstd"] = sb("rstd", [128, 1])
        s["t0f"] = sb("t0f", [128, C])
        s["ps_a"] = psum("ps_a", [128, 2, 512])
        s["ps_b"] = psum("ps_b", [128, 2, 512])
        s["ps_o"] = psum("ps_o", [128, 4, 512])

        def sem(name):
            return st.enter_context(nc.semaphore(name))

        block = st.enter_context(nc.Block())
        b.pe = Eng(nc.tensor, sem("pe_ch"), serialize=False)
        b.ve = Eng(nc.vector, sem("ve_ch"), serialize=True)
        b.ac = Eng(nc.scalar, sem("ac_ch"), serialize=True)
        b.po = Eng(nc.gpsimd, sem("po_ch"), serialize=True)
        b.ld = DmaQ(nc.sync)
        b.stq = DmaQ(nc.scalar)
        ldq, stqq = b.ld, b.stq
        b.streams = {}
        for nm in ["Lmisc", "Lwt", "Lx0", "Lx1", "Ldt", "Lmr", "Lxqs",
                   "Lwg0", "Lwg1", "Lxr0", "Lxr1"]:
            b.streams[nm] = Stream(ldq, sem(nm))
        for nm in ["Swsp", "Sxst0", "Sxst1", "Ss0", "Ss1", "Smr", "Sdsw",
                   "So0", "So1"]:
            b.streams[nm] = Stream(stqq, sem(nm))
        if True:
            b.s = s
            b.g = g
            b._ksq_free_pe = 0
            b._qsq_free_pe = 0

            @block.sync
            def _(sync):
                ph = getattr(b, "phases", "WAF")
                with ExitStack() as stA:
                    def sba(name, shape, dt=F32):
                        return stA.enter_context(
                            nc.sbuf_tensor(name, shape, dt))
                    s["greps"] = sba("greps", [128, 4, C])
                    s["qs_aa"] = sba("qs_aa", [128, 1024])
                    s["qs_c0"] = sba("qs_c0", [128, 1024])
                    s["qs_c1"] = sba("qs_c1", [128, 1024])
                    s["qs_m"] = sba("qs_m", [128, 3, 64])
                    s["qs_d"] = sba("qs_d", [128, 64])
                    s["wtile"] = sba("wtile", [128, 2816])
                    s["wcast"] = sba("wcast", [128, 2, 2816], BF16)
                    s["w_qT"] = sba("w_qT", [128, NCC, C], BF16)
                    s["w_kvT"] = sba("w_kvT", [128, NCC, 128], BF16)
                    s["w_oT"] = sba("w_oT", [128, NCC, C], BF16)
                    s["KT"] = sba("KT", [65, T], BF16)
                    s["VhX"] = sba("VhX", [128, NT, 192], BF16)
                    s["QT"] = sba("QT", [65, 16, 512], BF16)
                    s["xnlT"] = sba("xnlT", [128, NCC, 512], BF16)
                    s["AOn"] = sba("AOn", [128, NCC, 512], BF16)
                    s["expb"] = sba("expb", [128, 2, 2, 512], BF16)
                    s["rb"] = sba("rb", [128, 512])
                    s["aosb"] = sba("aosb", [128, 2, 2, 512])
                    s["dtsw"] = sba("dtsw", [128, 512])
                    s["qsq"] = sba("qsq", [64, 1024], BF16)
                    s["ksq"] = sba("ksq", [64, 128], BF16)
                    s["qn"] = sba("qn", [1, NCC, 512], BF16)
                    s["x_t"] = sba("x_t", [128, 2, C])
                    s["xn_t"] = sba("xn_t", [128, C])
                    s["xnl_t"] = sba("xnl_t", [128, 2, C], BF16)
                    s["xq_t"] = sba("xq_t", [128, 2, C], BF16)
                    s["xqTst"] = sba("xqTst", [128, 3, NCC, 128], BF16)
                    s["vtmp"] = sba("vtmp", [64, 128], BF16)
                    s["st_s"] = sba("st_s", [128, 2, C], BF16)
                    b.emit_w()
                    if "A" in ph:
                        b.emit_a()
                with ExitStack() as stF:
                    def sbf(name, shape, dt=F32):
                        return stF.enter_context(
                            nc.sbuf_tensor(name, shape, dt))
                    s["qs_aa"] = sbf("f_qs_aa", [128, 512])
                    s["qs_c0"] = sbf("f_qs_c0", [128, 512])
                    s["qs_c1"] = sbf("f_qs_c1", [128, 512])
                    s["qs_m"] = sbf("f_qs_m", [128, 3, 32])
                    s["qs_d"] = sbf("f_qs_d", [128, 32])
                    s["wqd_sb"] = sbf("wqd_sb", [128, NJT, C], BF16)
                    s["hqT"] = sbf("hqT", [128, NJT, 1024], BF16)
                    s["xqT_S"] = sbf("xqT_S", [128, NCC, 1024], BF16)
                    s["wgb"] = sbf("wgb", [128, 2, NCC, 512], BF16)
                    s["wub"] = sbf("wub", [128, 2, NCC, 512], BF16)
                    s["hg"] = sbf("hg", [128, 512])
                    s["hsb"] = sbf("hsb", [128, 512])
                    s["hq"] = sbf("hq", [128, 2, 512], BF16)
                    s["g_sb"] = sbf("g_sb", [128, C])
                    s["s_ld"] = sbf("s_ld", [128, 2, C], BF16)
                    s["xr"] = sbf("xr", [128, 2, C])
                    s["o_sb"] = sbf("o_sb", [128, 2, C])
                    if "F" in ph:
                        b.emit_f()
                    else:
                        doq = b.streams["So0"]
                        for t in range(b.NT):
                            nc.scalar.wait_ge(b.ve.sem, b.ve.n)
                            doq.dma(g["out_d"][sl(t), :], g["x_d"][sl(t), :])
                        nc.sync.wait_ge(doq.sem, 16 * doq.n)

    return nc


def kernel(**inputs):
    T = 4096
    nc = build_nc(T)
    x = np.ascontiguousarray(np.asarray(inputs["L0_attn_x"], dtype=np.float32))
    weights = {k: np.ascontiguousarray(np.asarray(v, dtype=np.float32))
               for k, v in inputs.items() if k != "L0_attn_x"}
    in_maps = []
    for bb in range(N_CORES):
        m = {"L0_attn_x": x[bb]}
        m.update(weights)
        in_maps.append(m)
    res = run_bass_kernel_spmd(nc, in_maps, core_ids=list(range(N_CORES)))
    return np.stack([res.results[bb]["out"] for bb in range(N_CORES)], axis=0)


# version 19 (final)
# speedup vs baseline: 1.0930x; 1.0087x over previous
import math
import numpy as np
import concourse.bass as bass
import concourse.mybir as mybir
from concourse.bass_utils import run_bass_kernel_spmd

F32 = mybir.dt.float32
BF16 = mybir.dt.bfloat16
I32 = mybir.dt.int32
AF = mybir.ActivationFunctionType
ALU = mybir.AluOpType
AX = mybir.AxisListType

C = 1024
J = 2816
NCC = 8
NJT = 22
QSC = float(np.float32(0.125 * math.sqrt(0.125)))
KSC = float(np.float32(math.sqrt(0.125)))
EPS = 1e-5
N_CORES = 8


def sl(i, n=128):
    return slice(i * n, (i + 1) * n)


class Eng:
    def __init__(self, eng, semh, serialize):
        self.e = eng
        self.sem = semh
        self.n = 0
        self.serialize = serialize

    def wait(self, other, val):
        if val > 0:
            self.e.wait_ge(other.sem, val)

    def wait_dma(self, dmaq, n):
        if n > 0:
            self.e.wait_ge(dmaq.sem, 16 * n)

    def op(self, name, *a, **k):
        if self.serialize and self.n > 0:
            self.e.wait_ge(self.sem, self.n)
        r = getattr(self.e, name)(*a, **k)
        r.then_inc(self.sem, 1)
        self.n += 1
        return self.n


class DmaQ:

    def __init__(self, eng):
        self.e = eng

    def wait(self, other, val):
        if val > 0:
            self.e.wait_ge(other.sem, val)

    def wait_dma(self, stream, n):
        if n > 0:
            self.e.wait_ge(stream.sem, 16 * n)


class Stream:

    def __init__(self, q, semh):
        self.q = q
        self.sem = semh
        self.n = 0

    def dma(self, out, in_, **k):
        self.q.e.dma_start(out=out, in_=in_, **k).then_inc(self.sem, 16)
        self.n += 1
        return self.n


class Builder:
    def __init__(self, nc, T):
        self.nc = nc
        self.T = T
        self.NT = T // 128
        self.NSUP = T // 512
        self.NFS = T // 1024
        self._tr_free = [(None, 0), (None, 0)]
        self._sg_free = 0
        self._wcast_free_pe = [0, 0]
        self._psb0_free_ve = 0
        self._psb0_free_ac = 0
        self._psb1_free_ve = 0
        self._qp_free_ve = 0
        self._qp_free_ac = 0
        self._ao_free = None
        self.aosb_free_ve = [0, 0]
        self._expb_free_pe = [0, 0]
        self._sim_free_ac = [0, 0]
        self._dsw_free = 0
        self._dtsw_free = 0
        self._rb_free = 0
        self._ln_ac_free = 0
        self._ln_ve_free = 0
        self._htr_free = 0
        self._ld_dt = 0
        self.wffn_parts = []
        self._pq_small_free = 0

    def tile_barrier(self):
        gp = self.pe.n
        self.ve.wait(self.pe, gp)
        self.ac.wait(self.pe, gp)

    def quant(self, v_ap, out_ap, F, v_ve=0, v_dma=0, out_gate_pe=0,
              eng=None, pq=False):
        ve, s = self.ve, self.s
        e = eng if eng is not None else ve
        pfx = "pq_" if pq else "qs_"
        nb = F // 16
        aa = s[pfx + "aa"][:, :F]
        cx = s[pfx + "c0"][:, :F]
        cy = s[pfx + "c1"][:, :F]
        mm = s[pfx + "m"][:, 0, :nb]
        sc = s[pfx + "m"][:, 1, :nb]
        rcp = s[pfx + "m"][:, 2, :nb]
        dqs = s[pfx + "d"][:, :nb]
        v3 = v_ap.rearrange("p (b i) -> p b i", i=16)

        if v_dma:
            ve.wait_dma(v_dma[0], v_dma[1])
        if e is not ve:
            ve.wait(e, self._pq_small_free)
        ve.op("tensor_reduce", mm, v3, AX.X, ALU.max,
              apply_absolute_value=True)
        ve.op("tensor_scalar", sc, mm, s["rcp6"][:, 0:1], 1e-12,
              ALU.mult, ALU.max)
        ve.op("reciprocal", rcp, sc)
        nved = ve.op("tensor_copy", dqs, sc)
        if e is not ve:
            e.wait(ve, nved)
            if v_ve:
                e.wait(ve, v_ve)
        e.op("tensor_tensor", cx.rearrange("p (b i) -> p b i", i=16), v3,
             rcp.unsqueeze(2).broadcast_to([128, nb, 16]), ALU.mult)
        if e is ve:
            e.op("tensor_scalar", aa.bitcast(I32), cx.bitcast(I32),
                 s["imask"][:, 0:1], None, ALU.bitwise_and)
        else:
            e.op("tensor_scalar", aa.bitcast(I32), cx.bitcast(I32),
                 -2147483648.0, None, ALU.bitwise_xor)
            e.op("tensor_tensor", aa, aa, cx, ALU.max)
        e.op("tensor_scalar", cy, aa, 4194305.0, None, ALU.mult)
        e.op("tensor_tensor", cx, cy, aa, ALU.subtract)
        e.op("tensor_tensor", cy, cy, cx, ALU.subtract)
        e.op("scalar_tensor_tensor", cy, aa, 0.75, cy, ALU.is_gt, ALU.mult)
        e.op("tensor_scalar", cx, aa, 0.25, None, ALU.is_gt)
        e.op("scalar_tensor_tensor", cx, aa, 0.75, cx, ALU.is_gt, ALU.add)
        e.op("scalar_tensor_tensor", cy, cx, 0.5, cy, ALU.mult, ALU.max)
        smk = (s["smask"][:, 0:1] if e is ve else -2147483648.0)
        e.op("scalar_tensor_tensor", cx.bitcast(I32), v_ap.bitcast(I32),
             smk, cy.bitcast(I32), ALU.bitwise_and,
             ALU.bitwise_or)
        if out_gate_pe:
            e.wait(self.pe, out_gate_pe)
        n = e.op("tensor_tensor",
                 out_ap.rearrange("p (b i) -> p b i", i=16),
                 cx.rearrange("p (b i) -> p b i", i=16),
                 dqs.unsqueeze(2).broadcast_to([128, nb, 16]), ALU.mult)
        if e is not ve:
            self._pq_small_free = n
        return n

    def ln_apply(self, v_ap, out_ap, F):
        ve, ac, s = self.ve, self.ac, self.s
        ng = (F + 511) // 512
        ve.wait(ac, self._ln_ac_free)
        for gi in range(ng):
            ve.op("bn_stats", s["bn6"][:, gi, :], v_ap[:, sl(gi, 512)])
        nv = ve.op("bn_aggr", s["st2"][:], s["bn6"][:, :ng, :])
        ac.wait(ve, nv)
        self._ln_ac_free = ac.op("activation", s["lnv"][:], s["st2"][:, 1:2],
                                 AF.Ln, bias=s["epsT"][:, 0:1])
        ac.wait(ve, self._ln_ve_free)
        na = ac.op("activation", s["rstd"][:], s["lnv"][:], AF.Exp, scale=-0.5)
        ve.wait(ac, na)
        napp = ve.op("tensor_scalar", out_ap, v_ap, s["st2"][:, 0:1],
                     s["rstd"][:], ALU.subtract, ALU.mult)
        self._ln_ve_free = napp
        return napp

    def transpose_chunks(self, src_ap, nch, dst_fn, pe_gate_ve,
                         gate_ac=None, gate_po=None, evac=None):
        pe, ve, s = self.pe, self.ve, self.s
        ev = evac if evac is not None else ve
        pe.wait(ve, pe_gate_ve)
        if gate_ac is not None:
            pe.wait(self.ac, gate_ac)
        if gate_po is not None:
            pe.wait(self.po, gate_po)
        trA = s["ps_o"][:, 3, 0:64].bitcast(BF16)
        trB = s["ps_o"][:, 2, 0:64].bitcast(BF16)
        for ch in range(nch):
            tgt = trA if ch % 2 == 0 else trB
            _e, _n = self._tr_free[ch % 2]
            if _n:
                pe.wait(_e, _n)
            npt = pe.op("transpose", tgt[:, 0:128], src_ap[:, sl(ch)],
                        s["identb"][:])
            ev.wait(pe, npt)
            if ev is self.ac:
                nv = ev.op("activation", dst_fn(ch), tgt[:, 0:128], AF.Copy)
            else:
                nv = ev.op("tensor_copy", dst_fn(ch), tgt[:, 0:128])
            self._tr_free[ch % 2] = (ev, nv)
        return ev.n, pe.n

    def emit_w(self):
        pe, ve, ac, po, ld, stq = (self.pe, self.ve, self.ac, self.po,
                                   self.ld, self.stq)
        s, g, st_ = self.s, self.g, self.streams

        po.op("memset", s["epsT"][:], EPS)
        po.op("memset", s["tiny1"][:], 1e-30)
        po.op("memset", s["imask"][:, 0:1], 0x7FFFFFFF)
        po.op("memset", s["smask"][:, 0:1], -2147483648)
        po.op("memset", s["identb"][:], 0.0)
        po.op("affine_select", out=s["identb"][:], in_=s["identb"][:],
              compare_op=ALU.not_equal, fill=1.0, base=0,
              pattern=[[-1, 128]], channel_multiplier=1)
        po.op("memset", s["triL"][:], 1.0)
        po.op("affine_select", out=s["triL"][:], in_=s["triL"][:],
              compare_op=ALU.is_ge, fill=0.0, base=0,
              pattern=[[1, 128]], channel_multiplier=-1)
        po.op("memset", s["negb"][:], -1e30)
        for j in range(4):
            po.op("affine_select", out=s["negb"][:, j, :],
                  in_=s["negb"][:, j, :], compare_op=ALU.is_ge, fill=0.0,
                  base=128 * j - 1, pattern=[[-1, 512]], channel_multiplier=1)
        po.op("memset", s["VhX"][:, :, 64:128], 1.0)
        po.op("memset", s["KT"][64:65, :], 1.0)
        po.op("memset", s["ones64"][:], 1.0)
        po.op("memset", s["ones128"][:], 1.0)
        po.op("memset", s["k2r"][:], 0.0)
        po.op("memset", s["rcp6"][:, 1:2], 6.0)
        n_po_w = po.n
        ve.wait(po, n_po_w)
        ve.op("reciprocal", s["rcp6"][:, 0:1], s["rcp6"][:, 1:2])

        for i, t in enumerate([g["lng_d"], g["lnb_d"], g["lnog_d"],
                               g["lnob_d"]]):
            st_["Lmisc"].dma(s["greps"][:, i, :],
                             t.unsqueeze(0).broadcast_to([128, C]))
        self._n_greps = st_["Lmisc"].n

        ve.wait(po, n_po_w)
        ac.wait(po, n_po_w)
        pe.wait(po, n_po_w)
        self._wtile_free = 0

        def cast_transpose(src_dram, row_tiles, dst, scale_rows=None):
            for rt in range(row_tiles):
                ld.wait(ve, self._wtile_free)
                nl = st_["Lwt"].dma(s["wtile"][:, :C], src_dram[sl(rt), :])
                ve.wait_dma(st_["Lwt"], nl)
                if scale_rows is None:
                    ncst = ve.op("tensor_copy", s["wcast"][:, 0, :C],
                                 s["wtile"][:, :C])
                else:
                    lo, hi, sc_ = scale_rows
                    ncst = ve.op("tensor_scalar", s["wcast"][lo:hi, 0, :C],
                                 s["wtile"][lo:hi, :C], sc_, None, ALU.mult)
                    if hi < 128:
                        ncst = ve.op("tensor_copy", s["wcast"][hi:128, 0, :C],
                                     s["wtile"][hi:128, :C])
                self._wtile_free = ncst
                _, npe0 = self.transpose_chunks(
                    s["wcast"][:, 0, :C], NCC, lambda cc: dst[:, cc, sl(rt)],
                    ncst)
                self._wcast_free_pe[0] = npe0

        cast_transpose(g["wq_d"], NCC, s["w_qT"], scale_rows=(0, 128, QSC))
        cast_transpose(g["wkv_d"], 1, s["w_kvT"], scale_rows=(0, 64, KSC))
        cast_transpose(g["wo_d"], NCC, s["w_oT"])

        self._wstage_free = 0
        self._wtile_free_po = 0

        def w_tile_ve(src, rt, rowlen, dstT):
            wslot = self._wslot
            self._wslot ^= 1
            ld.wait(ve, self._wtile_free_po)
            nl = st_["Lwt"].dma(s["wtile"][:, :rowlen], src[sl(rt), :])
            ve.wait(pe, self._wcast_free_pe[wslot])
            for off in range(0, rowlen, 1024):
                w = min(1024, rowlen - off)
                self.quant(s["wtile"][:, off:off + w],
                           s["wcast"][:, wslot, off:off + w], w,
                           v_dma=(st_["Lwt"], nl))
            self._wtile_free_po = ve.n
            return (rt, rowlen, dstT, wslot, ve.n)

        def w_tile_pe(pend):
            rt, rowlen, dstT, wslot, nq = pend
            nch = rowlen // 128
            for glo in range(0, nch, 8):
                ghi = min(glo + 8, nch)
                ac.wait_dma(st_["Swsp"], self._wstage_free)
                nev, npe = self.transpose_chunks(
                    s["wcast"][:, wslot, glo * 128:ghi * 128], ghi - glo,
                    lambda ch: s["xqTst"][:, 2, ch, :], nq, evac=ac)
                self._wcast_free_pe[wslot] = npe
                stq.wait(ac, nev)
                self._wstage_free = st_["Swsp"].dma(
                    dstT[:, glo:ghi, sl(rt)],
                    s["xqTst"][:, 2, 0:ghi - glo, :])

        self._w_tile_ve = w_tile_ve
        self._w_tile_pe = w_tile_pe
        self._w_pend = None
        self._wslot = 0
        self.wffn_parts = (
            [(g["wg_d"], rt, C, g["wqgT_s"]) for rt in range(NJT)] +
            [(g["wu_d"], rt, C, g["wquT_s"]) for rt in range(NJT)] +
            [(g["wd_d"], rt, J, g["wqdT_s"]) for rt in range(NCC)])

    def p0_ve_part(self, t):
        ve, ld = self.ve, self.ld
        s, g, st_ = self.s, self.g, self.streams
        slot = t % 2
        ld.wait(ve, self.xt_free[slot])
        nx = st_["Lx%d" % slot].dma(s["x_t"][:, slot, :], g["x_d"][sl(t), :])
        ve.wait_dma(st_["Lx%d" % slot], nx)
        nxn = self.ln_apply(s["x_t"][:, slot, :], s["xn_t"][:], C)
        self.xt_free[slot] = nxn
        self.quant(s["xn_t"][:], s["xq_t"][:, slot, :], C, v_ve=nxn,
                   out_gate_pe=self._xqt_pe[slot])
        self.ln_apply(s["xn_t"][:], s["t0f"][:], C)
        ve.op("tensor_tensor", s["t0f"][:], s["t0f"][:],
              s["greps"][:, 0, :], ALU.mult)
        ve.wait(self.pe, self._xnlt_pe[slot])
        return ve.op("tensor_tensor", s["xnl_t"][:, slot, :], s["t0f"][:],
                     s["greps"][:, 1, :], ALU.add)

    def p0_pe_part(self, t, nxnl):
        pe, ve, ac, ld, stq = self.pe, self.ve, self.ac, self.ld, self.stq
        s, g, st_ = self.s, self.g, self.streams
        ps_o = s["ps_o"]
        slot = t % 2
        tt = t % 4
        nq = nxnl
        ve.wait_dma(st_["Sxst%d" % slot], self.xst_free[slot])
        nev, self._xqt_pe[slot] = self.transpose_chunks(
            s["xq_t"][:, slot, :], NCC,
            lambda cc, _b=slot: s["xqTst"][:, _b, cc, :], nq)
        stq.wait(ve, nev)
        self.xst_free[slot] = st_["Sxst%d" % slot].dma(
            g["xqT_s"][:, :, sl(t)], s["xqTst"][:, slot, :, :])
        nevl, self._xnlt_pe[slot] = self.transpose_chunks(
            s["xnl_t"][:, slot, :], NCC,
            lambda cc, _tt=tt: s["xnlT"][:, cc, sl(_tt)], nxnl)
        kvp = ps_o[:, 2, 128:256]
        pe.wait(ve, nevl)
        _e, _n = self._tr_free[1]
        if _n:
            pe.wait(_e, _n)
        for cc in range(NCC):
            nkv = pe.op("matmul", kvp, s["w_kvT"][:, cc, :],
                        s["xnlT"][:, cc, sl(tt)],
                        start=(cc == 0), stop=(cc == NCC - 1))
        ac.wait(pe, nkv)
        nktc = ac.op("activation", s["KT"][0:64, sl(t)], kvp[0:64, :],
                     AF.Copy)
        nvt = ac.op("activation", s["vtmp"][:], kvp[64:128, :], AF.Copy)
        ac.wait(pe, self._ksq_free_pe)
        nks = ac.op("activation", s["ksq"][:], s["KT"][0:64, sl(t)],
                    AF.Square)
        k2p = ps_o[0:1, 2, 256:384]
        pe.wait(ac, nks)
        pe.wait(ve, self._k2_evac)
        nk2m = pe.op("matmul", k2p, s["ones64"][:, 0:1], s["ksq"][:],
                     start=True, stop=True)
        self._ksq_free_pe = nk2m
        ve.wait(pe, nk2m)
        ve.op("reduce_max", s["smalls"][0:1, 0:1], k2p, AX.X)
        ve.wait(ac, self._k2r_ac)
        self._k2_evac = ve.op(
            "tensor_scalar", s["k2r"][0:1, 0:1], s["k2r"][0:1, 0:1],
            s["smalls"][0:1, 0:1], None, ALU.max)
        self.n_k2 = self._k2_evac
        vtr = ps_o[:, 2, 384:448].bitcast(BF16)
        pe.wait(ve, self._k2_evac)
        npv = pe.op("transpose", vtr[:, 0:64], s["vtmp"][:],
                    s["identb"][0:64, 0:64])
        ve.wait(pe, npv)
        ve.op("tensor_copy", s["VhX"][:, t, 0:64], vtr[:, 0:64])
        self._tr_free[1] = (ve, ve.op("tensor_copy",
                                      s["VhX"][:, t, 128:192],
                                      vtr[:, 0:64]))

    def emit_qhat(self, sup):
        pe, ve, ac, ld, stq = self.pe, self.ve, self.ac, self.ld, self.stq
        s, g, st_ = self.s, self.g, self.streams
        ps_a, ps_b = s["ps_a"], s["ps_b"]
        ac.wait(ve, self.n_k2)
        nlnk2 = ac.op("activation", s["k2r"][0:1, 1:2],
                      s["k2r"][0:1, 0:1], AF.Ln,
                      bias=s["tiny1"][0:1, 0:1])
        self._k2r_ac = nlnk2
        qn_evac = 0
        pe.wait(ac, self._sim_free_ac[0])
        pe.wait(ac, self._sim_free_ac[1])
        for dd in range(NCC):
            qp = ps_a[:, 0, :]
            pe.wait(ac, self._qp_free_ve)
            for cc in range(NCC):
                nqm = pe.op("matmul", qp, s["w_qT"][:, cc, sl(dd)],
                            s["xnlT"][:, cc, :],
                            start=(cc == 0), stop=(cc == NCC - 1))
            ac.wait(pe, nqm)
            ac.op("activation", s["QT"][0:64, 2 * dd, :], qp[0:64, :],
                  AF.Copy)
            nqe = ac.op("activation", s["QT"][0:64, 2 * dd + 1, :],
                        qp[64:128, :], AF.Copy)
            self._qp_free_ve = nqe
            ac.wait(pe, self._qsq_free_pe)
            nqs = ac.op(
                "activation", s["qsq"][:].rearrange("p (h q) -> p h q",
                                                    q=512),
                s["QT"][0:64, 2 * dd:2 * dd + 2, :], AF.Square)
            qnp = ps_b[0:1, 0, :]
            pe.wait(ac, nqs)
            pe.wait(ve, self._psb0_free_ve)
            pe.op("matmul", qnp, s["ones64"][:, 0:1],
                  s["qsq"][:, 0:512], start=True, stop=False)
            nqn = pe.op("matmul", qnp, s["ones64"][:, 0:1],
                        s["qsq"][:, 512:1024], start=False, stop=True)
            self._qsq_free_pe = nqn
            ve.wait(pe, nqn)
            if dd == 0:
                ve.wait_dma(st_["Smr"], self.nmd)
            qn_evac = ve.op("tensor_copy", s["qn"][0:1, dd, :], qnp)
            self._psb0_free_ve = qn_evac
        qf = s["qn"][0:1, :, :].rearrange("a b c -> a (b c)")
        ac.wait(ve, qn_evac)
        nmr1 = ac.op("activation", qf, qf, AF.Ln,
                     bias=s["tiny1"][0:1, 0:1])
        ve.wait(ac, max(nmr1, nlnk2))
        nmr2 = ve.op("tensor_scalar", qf, qf, s["k2r"][0:1, 1:2], None,
                     ALU.add)
        ac.wait(ve, nmr2)
        nmr3 = ac.op("activation", qf, qf, AF.Exp, scale=0.5)
        ve.wait(ac, nmr3)
        nmrb = ve.op("tensor_scalar", qf, qf, -1.0, None, ALU.mult)
        stq.wait(ve, nmrb)
        stq.wait_dma(st_["Lmr"], self.nmq)
        self.nmd = st_["Smr"].dma(g["mrow_d"].unsqueeze(0),
                                  s["qn"][0:1, :, :])
        ld.wait_dma(st_["Smr"], self.nmd)
        ld.wait(pe, self._pe_after_sweeps)
        self.nmq = st_["Lmr"].dma(
            s["QT"][64:65, :, :].rearrange("o (p h) q -> o p h q", h=2),
            g["mrow_d"].unsqueeze(1).broadcast_to([NCC, 2, 512])
            .unsqueeze(0))
        self._preA_ve = ve.n

    def emit_sweeps(self, sup, inject=None):
        pe, ve, ac, ld, stq = self.pe, self.ve, self.ac, self.ld, self.stq
        po = self.po
        s, g, st_ = self.s, self.g, self.streams
        ps_a, ps_b, ps_o = s["ps_a"], s["ps_b"], s["ps_o"]
        kvmax = 4 * (sup + 1)
        pe.wait_dma(st_["Lmr"], self.nmq)
        sv_ve = self._preA_ve
        for w in range(NCC):
            if inject is not None and w < len(inject) and inject[w]:
                inject[w]()
            h0, h1 = 2 * w, 2 * w + 1
            ao0 = ps_o[:, 0, :]
            ao1 = ps_o[:, 1, :]
            npvm = 0
            pend = None

            def emit_pv(k, nex, pv_gate_ve):
                pe.wait(ac, nex)
                if pv_gate_ve:
                    pe.wait(ve, pv_gate_ve)
                if k == 0:
                    _e, _n = self._ao_free
                    pe.wait(_e, _n)
                ebuf = s["expb"][:, k % 2, :, :]
                pe.op("matmul", ao0, s["VhX"][:, k, 0:128],
                      ebuf[:, 0, :], start=(k == 0),
                      stop=(k == kvmax - 1), skip_group_check=True)
                n = pe.op("matmul", ao1, s["VhX"][:, k, 64:192],
                          ebuf[:, 1, :], start=(k == 0),
                          stop=(k == kvmax - 1), skip_group_check=True)
                self._expb_free_pe[k % 2] = n
                return n

            for k in range(kvmax):
                simp = ps_a if k % 2 == 0 else ps_b
                pe.wait(ac, self._sim_free_ac[k % 2])
                if w == 0 and k < 2:
                    pe.wait(ve, sv_ve)
                diag = k >= 4 * sup
                jj = k - 4 * sup
                pe.op("matmul", simp[:, 0, :], s["KT"][:, sl(k)],
                      s["QT"][:, h0, :], start=True, stop=not diag)
                if diag:
                    pe.op("matmul", simp[:, 0, :], s["triL"][:],
                          s["negb"][:, jj, :], start=False, stop=True)
                pe.op("matmul", simp[:, 1, :], s["KT"][:, sl(k)],
                      s["QT"][:, h1, :], start=True, stop=not diag)
                if diag:
                    pe.op("matmul", simp[:, 1, :], s["triL"][:],
                          s["negb"][:, jj, :], start=False, stop=True)
                nqk = pe.n
                if pend is not None:
                    npvm = emit_pv(*pend)
                ac.wait(pe, nqk)
                ac.wait(pe, self._expb_free_pe[k % 2])
                ebuf = s["expb"][:, k % 2, :, :]
                nex = ac.op("activation",
                            ebuf.rearrange("p h q -> p (h q)"),
                            simp[:, :, :].rearrange("p h q -> p (h q)"),
                            AF.Exp)
                self._sim_free_ac[k % 2] = nex
                pend = (k, nex, 0)
            npvm = emit_pv(*pend)
            wsl = w % 2
            aob = s["aosb"][:, wsl, :, :]
            ac.wait(pe, npvm)
            ac.wait(ve, self.aosb_free_ve[wsl])
            ac.wait_dma(st_["Sdsw"], self._dsw_free)
            ac.op("activation", aob[:, 0, :], ao0, AF.Copy)
            naosb = ac.op("activation", aob[:, 1, :], ao1, AF.Copy)
            self._ao_free = (ac, naosb)
            ac.op("activation", s["rb"][64:128, :], aob[64:128, 0, :],
                  AF.Ln)
            ac.op("activation", s["rb"][0:64, :], aob[0:64, 1, :], AF.Ln)
            nrp = ac.op("activation", s["rb"][:], s["rb"][:], AF.Exp,
                        scale=-1.0)
            stq.wait(ac, nrp)
            stq.wait_dma(st_["Ldt"], self._ld_dt)
            st_["Sdsw"].dma(g["dsw_d"][64:128, :], s["rb"][64:128, :])
            ndw = st_["Sdsw"].dma(g["dsw_d"][0:64, :], s["rb"][0:64, :])
            self._dsw_free = ndw
            ld.wait_dma(st_["Sdsw"], ndw)
            ld.wait(ve, self._dtsw_free)
            st_["Ldt"].dma(s["dtsw"][0:64, :], g["dsw_d"][64:128, :])
            ndr = st_["Ldt"].dma(s["dtsw"][64:128, :],
                                 g["dsw_d"][0:64, :])
            self._ld_dt = ndr
            ve.wait_dma(st_["Ldt"], ndr)
            ve.wait(ac, naosb)
            ve.op("tensor_tensor", s["AOn"][0:64, w, :],
                  aob[0:64, 0, :], s["dtsw"][0:64, :], ALU.mult)
            naon = ve.op("tensor_tensor", s["AOn"][64:128, w, :],
                         aob[64:128, 1, :], s["dtsw"][64:128, :],
                         ALU.mult)
            self._dtsw_free = naon
            self.aosb_free_ve[wsl] = naon

        aon_all = ve.n
        for sub in range(4):
            gt = sup * 4 + sub
            op0 = ps_o[:, 0, :]
            op1 = ps_o[:, 1, :]
            pe.wait(ve, aon_all)
            pe.wait(ve, self._op_free_ve)
            for ee in range(NCC):
                pe.op("matmul", op0, s["AOn"][:, ee, sl(sub)],
                      s["w_oT"][:, ee, 0:512],
                      start=(ee == 0), stop=(ee == NCC - 1),
                      skip_group_check=True)
                nop = pe.op("matmul", op1, s["AOn"][:, ee, sl(sub)],
                            s["w_oT"][:, ee, 512:1024],
                            start=(ee == 0), stop=(ee == NCC - 1),
                            skip_group_check=True)
            ve.wait(pe, nop)
            opv = ps_o[:, 0:2, :].rearrange("p a b -> p (a b)")
            self.ln_apply(opv, s["t0f"][:], C)
            ve.op("tensor_tensor", s["t0f"][:], s["t0f"][:],
                  s["greps"][:, 2, :], ALU.mult)
            ve.wait_dma(st_["Ss%d" % (sub % 2)], self.sst_free[sub % 2])
            ns = ve.op("tensor_tensor", s["st_s"][:, sub % 2, :],
                       s["t0f"][:], s["greps"][:, 3, :], ALU.add)
            self._op_free_ve = ns
            stq.wait(ve, ns)
            self.sst_free[sub % 2] = st_["Ss%d" % (sub % 2)].dma(
                g["s_s"][sl(gt), :], s["st_s"][:, sub % 2, :])
        self._ao_free = (ve, self._op_free_ve)
        self._pe_after_sweeps = pe.n

    def emit_a(self):
        pe, ve, ac, po, ld, stq = (self.pe, self.ve, self.ac, self.po,
                                   self.ld, self.stq)
        s, st_ = self.s, self.streams
        NSUP = self.NSUP

        ve.wait_dma(st_["Lmisc"], self._n_greps)
        ve.wait_dma(st_["Swsp"], st_["Swsp"].n)
        self.xt_free = [0, 0]
        self.xst_free = [0, 0]
        self.sst_free = [0, 0]
        self._vtr_free = 0
        self._k2_evac = 0
        self._k2r_ac = 0
        self._op_free_ve = 0
        self._xqt_pe = [0, 0]
        self._xnlt_pe = [0, 0]
        self._pe_after_sweeps = 0
        self._preA_ve = 0
        self._ao_free = (ve, 0)
        self.aosb_free_ve = [0, 0]
        self.nmd = 0
        self.nmq = 0
        self.n_k2 = 0

        for sup in range(NSUP + 1):
            if sup == 0:
                for t in range(4):
                    nx = self.p0_ve_part(t)
                    self.p0_pe_part(t, nx)
            if sup >= 1:
                base = 4 * sup
                nx_hold = {}

                def mk_both(ttv, ttp, _b=base, _h=nx_hold, _last=(sup == NSUP)):
                    def f():
                        if not _last:
                            if ttv is not None:
                                _h[ttv] = self.p0_ve_part(_b + ttv)
                            if ttp is not None:
                                self.p0_pe_part(_b + ttp, _h[ttp])
                        if self._w_pend is not None:
                            self._w_tile_pe(self._w_pend)
                            self._w_pend = None
                        if self.wffn_parts:
                            self._w_pend = self._w_tile_ve(
                                *self.wffn_parts.pop(0))
                    return f

                inject = [mk_both(0, None), mk_both(1, 0),
                          mk_both(2, 1), mk_both(3, 2),
                          mk_both(None, 3), mk_both(None, None),
                          mk_both(None, None), mk_both(None, None)]
                self.emit_sweeps(sup - 1, inject)
            if sup < NSUP:
                self.emit_qhat(sup)
        while self.wffn_parts or self._w_pend is not None:
            if self._w_pend is not None:
                self._w_tile_pe(self._w_pend)
                self._w_pend = None
            if self.wffn_parts:
                self._w_pend = self._w_tile_ve(*self.wffn_parts.pop(0))

    def emit_f(self):
        pe, ve, ac, po, ld, stq = (self.pe, self.ve, self.ac, self.po,
                                   self.ld, self.stq)
        s, g, st_ = self.s, self.g, self.streams
        ps_a, ps_b, ps_o = s["ps_a"], s["ps_b"], s["ps_o"]
        NFS = self.NFS

        bar_pe, bar_ve, bar_ac = pe.n, ve.n, ac.n
        ld.wait(pe, bar_pe)
        ld.wait(ve, bar_ve)
        ld.wait(ac, bar_ac)
        for nm in ["Swsp", "Sxst0", "Sxst1", "Ss0", "Ss1", "Smr", "Sdsw"]:
            ld.wait_dma(st_[nm], st_[nm].n)
        pe.wait(ve, bar_ve)
        pe.wait(ac, bar_ac)
        ve.wait(pe, bar_pe)
        ac.wait(pe, bar_pe)

        nwd = st_["Lmisc"].dma(s["wqd_sb"][:], g["wqdT_s"][:, :, :])
        hq_read_pe = 0
        xq_read_pe = 0
        wg_free = [0, 0]
        xr_free = [0, 0]
        sld_free = [0, 0]
        o_free = [0, 0]
        g_done = 0

        hq_pe = [0, 0]
        for S in range(NFS):
            ld.wait(pe, xq_read_pe)
            nxq = st_["Lxqs"].dma(s["xqT_S"][:], g["xqT_s"][:, :, sl(S, 1024)])
            silu_done = 0
            tt_done = 0
            pend = None
            first_evac = True
            ui = 0
            for jc in range(6):
                jw = 512 if jc < 5 else 256
                jb = jc % 2
                ld.wait(pe, wg_free[jb])
                st_["Lwg%d" % jb].dma(
                    s["wgb"][:, jb, :, :jw],
                    g["wqgT_s"][:, :, jc * 512:jc * 512 + jw])
                nwu = st_["Lwg%d" % jb].dma(
                    s["wub"][:, jb, :, :jw],
                    g["wquT_s"][:, :, jc * 512:jc * 512 + jw])
                pe.wait_dma(st_["Lwg%d" % jb], nwu)
                if jc == 0:
                    pe.wait_dma(st_["Lxqs"], nxq)
                for sub in range(8):
                    slot = ui % 2
                    gp = ps_a[:, 0, :jw]
                    up = ps_b[:, 0, :jw]
                    pe.wait(ac, silu_done)
                    pe.wait(ve, tt_done)
                    for cc in range(NCC):
                        pe.op("matmul", gp, s["xqT_S"][:, cc, sl(sub)],
                              s["wgb"][:, jb, cc, :jw],
                              start=(cc == 0), stop=(cc == NCC - 1),
                              skip_group_check=True)
                        ngu = pe.op("matmul", up, s["xqT_S"][:, cc, sl(sub)],
                                    s["wub"][:, jb, cc, :jw],
                                    start=(cc == 0), stop=(cc == NCC - 1),
                                    skip_group_check=True)
                    wg_free[jb] = ngu
                    xq_read_pe = ngu
                    npt_prev = 0
                    if pend is not None:
                        pslot, pjw, pnq, pdst = pend
                        trC = ps_a[:, 1, 0:256].bitcast(BF16)
                        pe.wait(ve, pnq)
                        pe.wait(ac, self._htr_free)
                        for ji in range(pjw // 128):
                            npt_prev = pe.op("transpose", trC[:, sl(ji)],
                                             s["hq"][:, pslot, sl(ji)],
                                             s["identb"][:])
                        hq_pe[pslot] = npt_prev
                    ac.wait(pe, ngu)
                    ac.wait(ve, tt_done)
                    silu_done = ac.op("activation", s["hg"][:, :jw], gp,
                                      AF.Silu)
                    ve.wait(ac, silu_done)
                    ntt = ve.op("tensor_tensor", s["hsb"][:, :jw],
                                s["hg"][:, :jw], up, ALU.mult)
                    tt_done = ntt
                    nq = self.quant(s["hsb"][:, :jw], s["hq"][:, slot, :jw],
                                    jw, v_ve=ntt, out_gate_pe=hq_pe[slot])
                    if pend is not None:
                        pslot, pjw, pnq, pdst = pend
                        trC = ps_a[:, 1, 0:256].bitcast(BF16)
                        if first_evac:
                            ac.wait(pe, hq_read_pe)
                            first_evac = False
                        ac.wait(pe, npt_prev)
                        self._htr_free = ac.op(
                            "activation", pdst,
                            trC[:, 0:pjw].rearrange("p (a b) -> p a b",
                                                    b=128), AF.Copy)
                    pend = (slot, jw,  nq,
                            s["hqT"][:, jc * 4:jc * 4 + jw // 128, sl(sub)])
                    ui += 1
            pslot, pjw, pnq, pdst = pend
            trC = ps_a[:, 1, 0:256].bitcast(BF16)
            pe.wait(ve, pnq)
            pe.wait(ac, self._htr_free)
            for ji in range(pjw // 128):
                npt = pe.op("transpose", trC[:, sl(ji)],
                            s["hq"][:, pslot, sl(ji)], s["identb"][:])
            hq_pe[pslot] = npt
            if first_evac:
                ac.wait(pe, hq_read_pe)
            ac.wait(pe, npt)
            self._htr_free = ac.op(
                "activation", pdst,
                trC[:, 0:pjw].rearrange("p (a b) -> p a b", b=128), AF.Copy)
            nhq = ac.n
            pe.wait(ac, nhq)
            pe.wait_dma(st_["Lmisc"], nwd)
            for sub in range(8):
                gt = S * 8 + sub
                fb = sub % 2
                ld.wait(po, xr_free[fb])
                ld.wait(po, sld_free[fb])
                st_["Lxr%d" % fb].dma(s["xr"][:, fb, :], g["x_d"][sl(gt), :])
                nsl = st_["Lxr%d" % fb].dma(s["s_ld"][:, fb, :],
                                            g["s_s"][sl(gt), :])
                fp0 = ps_o[:, 0, :]
                fp1 = ps_o[:, 1, :]
                pe.wait(ac, g_done)
                for jcc in range(NJT):
                    pe.op("matmul", fp0, s["hqT"][:, jcc, sl(sub)],
                          s["wqd_sb"][:, jcc, 0:512],
                          start=(jcc == 0), stop=(jcc == NJT - 1),
                          skip_group_check=True)
                    nfm = pe.op("matmul", fp1, s["hqT"][:, jcc, sl(sub)],
                                s["wqd_sb"][:, jcc, 512:1024],
                                start=(jcc == 0), stop=(jcc == NJT - 1),
                                skip_group_check=True)
                hq_read_pe = nfm
                ac.wait(pe, nfm)
                ac.wait(po, sld_free[1 - fb])
                g_done = ac.op(
                    "activation", s["g_sb"][:],
                    ps_o[:, 0:2, :].rearrange("p a b -> p (a b)"), AF.Silu)
                po = self.po
                po.wait(ac, g_done)
                po.wait_dma(st_["Lxr%d" % fb], nsl)
                po.op("tensor_copy", s["t0f"][:], s["s_ld"][:, fb, :])
                po.op("tensor_tensor", s["t0f"][:], s["t0f"][:], s["g_sb"][:],
                      ALU.mult)
                sld_free[fb] = po.n
                po.wait_dma(st_["So%d" % fb], o_free[fb])
                no = po.op("tensor_tensor", s["o_sb"][:, fb, :], s["t0f"][:],
                           s["xr"][:, fb, :], ALU.add)
                xr_free[fb] = no
                stq.wait(po, no)
                o_free[fb] = st_["So%d" % fb].dma(g["out_d"][sl(gt), :],
                                                  s["o_sb"][:, fb, :])
        self.nc.sync.wait_ge(st_["So0"].sem, 16 * st_["So0"].n)
        self.nc.sync.wait_ge(st_["So1"].sem, 16 * st_["So1"].n)


def build_nc(T, phases="WAF"):
    nc = bass.Bass("TRN2", target_bir_lowering=False)
    b = Builder(nc, T)
    b.phases = phases[:3] if phases[:1] else phases
    if phases.endswith("n"):
        b.nointer = True
        phases = phases[:-1]
    if len(phases) > 2 and phases[2:3].isdigit():
        b.astage = int(phases[2])
        b.phases = "WA"
        if "t" in phases:
            b.tstage = int(phases.split("t")[1])
    else:
        b.phases = phases
    NT = b.NT

    g = {}
    g["x_d"] = nc.dram_tensor("L0_attn_x", [T, C], F32, kind="ExternalInput").ap()
    g["wq_d"] = nc.dram_tensor("L0_attn_w_q", [C, C], F32, kind="ExternalInput").ap()
    g["wkv_d"] = nc.dram_tensor("L0_attn_w_kv", [128, C], F32, kind="ExternalInput").ap()
    g["wo_d"] = nc.dram_tensor("L0_attn_w_out", [C, C], F32, kind="ExternalInput").ap()
    g["lng_d"] = nc.dram_tensor("L0_attn_ln_g", [C], F32, kind="ExternalInput").ap()
    g["lnb_d"] = nc.dram_tensor("L0_attn_ln_b", [C], F32, kind="ExternalInput").ap()
    g["lnog_d"] = nc.dram_tensor("L0_attn_lno_g", [C], F32, kind="ExternalInput").ap()
    g["lnob_d"] = nc.dram_tensor("L0_attn_lno_b", [C], F32, kind="ExternalInput").ap()
    g["wg_d"] = nc.dram_tensor("L0_ffn_w_gate", [J, C], F32, kind="ExternalInput").ap()
    g["wu_d"] = nc.dram_tensor("L0_ffn_w_up", [J, C], F32, kind="ExternalInput").ap()
    g["wd_d"] = nc.dram_tensor("L0_ffn_w_down", [C, J], F32, kind="ExternalInput").ap()
    g["out_d"] = nc.dram_tensor("out", [T, C], F32, kind="ExternalOutput").ap()
    g["wqgT_s"] = nc.dram_tensor("wqgT_s", [128, NCC, J], BF16, kind="Internal").ap()
    g["wquT_s"] = nc.dram_tensor("wquT_s", [128, NCC, J], BF16, kind="Internal").ap()
    g["wqdT_s"] = nc.dram_tensor("wqdT_s", [128, NJT, C], BF16, kind="Internal").ap()
    g["xqT_s"] = nc.dram_tensor("xqT_s", [128, NCC, T], BF16, kind="Internal").ap()
    g["s_s"] = nc.dram_tensor("s_s", [T, C], BF16, kind="Internal").ap()
    g["mrow_d"] = nc.dram_tensor("mrow_d", [NCC, 512], BF16, kind="Internal").ap()
    g["dsw_d"] = nc.dram_tensor("dsw_d", [128, 512], F32, kind="Internal").ap()

    from contextlib import ExitStack
    with ExitStack() as st:
        def sb(name, shape, dt=F32):
            return st.enter_context(nc.sbuf_tensor(name, shape, dt))

        def psum(name, shape, dt=F32):
            return st.enter_context(nc.psum_tensor(name, shape, dt))

        s = {}
        s["epsT"] = sb("epsT", [128, 1])
        s["tiny1"] = sb("tiny1", [1, 1])
        s["imask"] = sb("imask", [128, 1], I32)
        s["smask"] = sb("smask", [128, 1], I32)
        s["identb"] = sb("identb", [128, 128], BF16)
        s["triL"] = sb("triL", [128, 128], BF16)
        s["negb"] = sb("negb", [128, 4, 512], BF16)
        s["ones64"] = sb("ones64", [64, 1], BF16)
        s["ones128"] = sb("ones128", [128, 1], BF16)
        s["smalls"] = sb("smalls", [1, 8])
        s["rcp6"] = sb("rcp6", [128, 2])
        s["k2r"] = sb("k2r", [1, 2])
        s["bn6"] = sb("bn6", [128, 2, 6])
        s["st2"] = sb("st2", [128, 2])
        s["lnv"] = sb("lnv", [128, 1])
        s["rstd"] = sb("rstd", [128, 1])
        s["t0f"] = sb("t0f", [128, C])
        s["ps_a"] = psum("ps_a", [128, 2, 512])
        s["ps_b"] = psum("ps_b", [128, 2, 512])
        s["ps_o"] = psum("ps_o", [128, 4, 512])

        def sem(name):
            return st.enter_context(nc.semaphore(name))

        block = st.enter_context(nc.Block())
        b.pe = Eng(nc.tensor, sem("pe_ch"), serialize=False)
        b.ve = Eng(nc.vector, sem("ve_ch"), serialize=True)
        b.ac = Eng(nc.scalar, sem("ac_ch"), serialize=True)
        b.po = Eng(nc.gpsimd, sem("po_ch"), serialize=True)
        b.ld = DmaQ(nc.sync)
        b.stq = DmaQ(nc.scalar)
        ldq, stqq = b.ld, b.stq
        b.streams = {}
        for nm in ["Lmisc", "Lwt", "Lx0", "Lx1", "Ldt", "Lmr", "Lxqs",
                   "Lwg0", "Lwg1", "Lxr0", "Lxr1"]:
            b.streams[nm] = Stream(ldq, sem(nm))
        for nm in ["Swsp", "Sxst0", "Sxst1", "Ss0", "Ss1", "Smr", "Sdsw",
                   "So0", "So1"]:
            b.streams[nm] = Stream(stqq, sem(nm))
        if True:
            b.s = s
            b.g = g
            b._ksq_free_pe = 0
            b._qsq_free_pe = 0

            @block.sync
            def _(sync):
                ph = getattr(b, "phases", "WAF")
                with ExitStack() as stA:
                    def sba(name, shape, dt=F32):
                        return stA.enter_context(
                            nc.sbuf_tensor(name, shape, dt))
                    s["greps"] = sba("greps", [128, 4, C])
                    s["qs_aa"] = sba("qs_aa", [128, 1024])
                    s["qs_c0"] = sba("qs_c0", [128, 1024])
                    s["qs_c1"] = sba("qs_c1", [128, 1024])
                    s["qs_m"] = sba("qs_m", [128, 3, 64])
                    s["qs_d"] = sba("qs_d", [128, 64])
                    s["wtile"] = sba("wtile", [128, 2816])
                    s["wcast"] = sba("wcast", [128, 2, 2816], BF16)
                    s["w_qT"] = sba("w_qT", [128, NCC, C], BF16)
                    s["w_kvT"] = sba("w_kvT", [128, NCC, 128], BF16)
                    s["w_oT"] = sba("w_oT", [128, NCC, C], BF16)
                    s["KT"] = sba("KT", [65, T], BF16)
                    s["VhX"] = sba("VhX", [128, NT, 192], BF16)
                    s["QT"] = sba("QT", [65, 16, 512], BF16)
                    s["xnlT"] = sba("xnlT", [128, NCC, 512], BF16)
                    s["AOn"] = sba("AOn", [128, NCC, 512], BF16)
                    s["expb"] = sba("expb", [128, 2, 2, 512], BF16)
                    s["rb"] = sba("rb", [128, 512])
                    s["aosb"] = sba("aosb", [128, 2, 2, 512])
                    s["dtsw"] = sba("dtsw", [128, 512])
                    s["qsq"] = sba("qsq", [64, 1024], BF16)
                    s["ksq"] = sba("ksq", [64, 128], BF16)
                    s["qn"] = sba("qn", [1, NCC, 512], BF16)
                    s["x_t"] = sba("x_t", [128, 2, C])
                    s["xn_t"] = sba("xn_t", [128, C])
                    s["xnl_t"] = sba("xnl_t", [128, 2, C], BF16)
                    s["xq_t"] = sba("xq_t", [128, 2, C], BF16)
                    s["xqTst"] = sba("xqTst", [128, 3, NCC, 128], BF16)
                    s["vtmp"] = sba("vtmp", [64, 128], BF16)
                    s["st_s"] = sba("st_s", [128, 2, C], BF16)
                    b.emit_w()
                    if "A" in ph:
                        b.emit_a()
                with ExitStack() as stF:
                    def sbf(name, shape, dt=F32):
                        return stF.enter_context(
                            nc.sbuf_tensor(name, shape, dt))
                    s["qs_aa"] = sbf("f_qs_aa", [128, 512])
                    s["qs_c0"] = sbf("f_qs_c0", [128, 512])
                    s["qs_c1"] = sbf("f_qs_c1", [128, 512])
                    s["qs_m"] = sbf("f_qs_m", [128, 3, 32])
                    s["qs_d"] = sbf("f_qs_d", [128, 32])
                    s["wqd_sb"] = sbf("wqd_sb", [128, NJT, C], BF16)
                    s["hqT"] = sbf("hqT", [128, NJT, 1024], BF16)
                    s["xqT_S"] = sbf("xqT_S", [128, NCC, 1024], BF16)
                    s["wgb"] = sbf("wgb", [128, 2, NCC, 512], BF16)
                    s["wub"] = sbf("wub", [128, 2, NCC, 512], BF16)
                    s["hg"] = sbf("hg", [128, 512])
                    s["hsb"] = sbf("hsb", [128, 512])
                    s["hq"] = sbf("hq", [128, 2, 512], BF16)
                    s["g_sb"] = sbf("g_sb", [128, C])
                    s["s_ld"] = sbf("s_ld", [128, 2, C], BF16)
                    s["xr"] = sbf("xr", [128, 2, C])
                    s["o_sb"] = sbf("o_sb", [128, 2, C])
                    if "F" in ph:
                        b.emit_f()
                    else:
                        doq = b.streams["So0"]
                        for t in range(b.NT):
                            nc.scalar.wait_ge(b.ve.sem, b.ve.n)
                            doq.dma(g["out_d"][sl(t), :], g["x_d"][sl(t), :])
                        nc.sync.wait_ge(doq.sem, 16 * doq.n)

    return nc


def kernel(**inputs):
    T = 4096
    nc = build_nc(T)
    x = np.ascontiguousarray(np.asarray(inputs["L0_attn_x"], dtype=np.float32))
    weights = {k: np.ascontiguousarray(np.asarray(v, dtype=np.float32))
               for k, v in inputs.items() if k != "L0_attn_x"}
    in_maps = []
    for bb in range(N_CORES):
        m = {"L0_attn_x": x[bb]}
        m.update(weights)
        in_maps.append(m)
    res = run_bass_kernel_spmd(nc, in_maps, core_ids=list(range(N_CORES)))
    return np.stack([res.results[bb]["out"] for bb in range(N_CORES)], axis=0)
